# version 12
# baseline (speedup 1.0000x reference)
import math
import numpy as np
from contextlib import ExitStack

import concourse.bass as bass
import concourse.mybir as mybir
from concourse.bass_utils import run_bass_kernel_spmd

F32 = mybir.dt.float32
AF = mybir.ActivationFunctionType
ALU = mybir.AluOpType

NEG = -1e30
EXP_SHIFT = -12.0
LN_EPS = 1e-5
ROPE_THETA = 10000.0

CFG_REAL = dict(D=4096, DKV=1024, S=4096, B=4, W=1024, R=64, TT=512)

ENGS = ("tensor", "vector", "scalar", "gpsimd", "sync")
SEM_SEG = 48000


class _Op:
    __slots__ = ("eng", "fn", "deps", "dma", "gidx", "marked", "sem", "val", "eidx")

    def __init__(self, eng, fn, deps, dma, gidx):
        self.eng, self.fn, self.deps, self.dma, self.gidx = eng, fn, list(deps), dma, gidx
        self.marked = False
        self.sem = None
        self.val = None
        self.eidx = -1


class PB:

    def __init__(self, nc):
        self.nc = nc
        self.ops = []
        self.eng_ops = {e: [] for e in ENGS}
        self._pending_barrier = {e: [] for e in ENGS}

    def add(self, eng, fn, deps=(), dma=False):
        deps = [d for d in deps if d is not None]
        if self._pending_barrier[eng]:
            deps = deps + self._pending_barrier[eng]
            self._pending_barrier[eng] = []
        op = _Op(eng, fn, deps, dma, len(self.ops))
        self.ops.append(op)
        self.eng_ops[eng].append(op)
        op.eidx = len(self.eng_ops[eng]) - 1
        return op

    def barrier(self):
        last = []
        for e in ENGS:
            ops = self.eng_ops[e]
            comp = [op for op in ops if not op.dma]
            dmas = [op for op in ops if op.dma]
            if comp:
                last.append(comp[-1])
            last.extend(dmas[-8:])
        for e in ENGS:
            self._pending_barrier[e] = list(last)

    def emit(self):
        nc = self.nc
        for op in self.ops:
            for d in op.deps:
                if d.eng != op.eng:
                    d.marked = True
        NPOOL = 8
        self.pre_wait = {}
        for e in ENGS:
            cnt, sem, nseg = 0, None, 0
            for op in self.eng_ops[e]:
                if op.dma or not op.marked:
                    continue
                if sem is None or cnt + 1 > SEM_SEG:
                    sem = nc.alloc_semaphore(f"s_{e}_c{nseg}")
                    nseg += 1
                    cnt = 0
                cnt += 1
                op.sem, op.val = sem, cnt
            dmas = [op for op in self.eng_ops[e] if op.dma]
            if not dmas:
                continue
            pool = [nc.alloc_semaphore(f"s_{e}_d{i}") for i in range(min(NPOOL, len(dmas)))]
            pcnt = [0] * len(pool)
            for i, op in enumerate(dmas):
                k = i % len(pool)
                pcnt[k] += 16
                op.sem, op.val = pool[k], pcnt[k]
                if op.val > 16:
                    self.pre_wait[op.gidx] = (op.sem, op.val - 16)

        blk_ctx = nc.Block()
        blk = blk_ctx.__enter__()

        def section(eng_name):
            def body(eng):
                waits = {}
                last_drain = -1
                for op in self.eng_ops[eng_name]:
                    need_drain = False
                    wmap = {}
                    if op.gidx in self.pre_wait:
                        s, v = self.pre_wait[op.gidx]
                        wmap[s] = v
                    for d in op.deps:
                        if d.eng == eng_name and not d.dma:
                            if d.eidx > last_drain:
                                need_drain = True
                        else:
                            assert d.sem is not None, (d.eng, d.dma, d.gidx)
                            key = d.sem
                            if wmap.get(key, -1) < d.val:
                                wmap[key] = d.val
                    if need_drain:
                        eng.drain()
                        last_drain = op.eidx - 1
                    for sem, val in wmap.items():
                        if waits.get(sem, -1) < val:
                            eng.wait_ge(sem, val)
                            waits[sem] = val
                    inst = op.fn(eng)
                    if op.sem is not None:
                        inst.then_inc(op.sem, 16 if op.dma else 1)
                return body
            return body

        blk.tensor(section("tensor"))
        blk.vector(section("vector"))
        blk.scalar(section("scalar"))
        blk.gpsimd(section("gpsimd"))
        blk.sync(section("sync"))
        blk_ctx.__exit__(None, None, None)


class Trk:

    def __init__(self, n=1):
        self.w = [None] * n
        self.r = [[] for _ in range(n)]

    def rd(self, slot=0):
        return [self.w[slot]] if self.w[slot] is not None else []

    def wr(self, slot=0):
        d = self.r[slot] + ([self.w[slot]] if self.w[slot] is not None else [])
        return d

    def did_rd(self, op, slot=0):
        self.r[slot].append(op)

    def did_wr(self, op, slot=0):
        self.w[slot] = op
        self.r[slot] = []


def _derived(cfg):
    d = dict(cfg)
    D, DKV, S, W, TT = cfg["D"], cfg["DKV"], cfg["S"], cfg["W"], cfg["TT"]
    d["T"] = S // 2
    d["KV"] = d["T"] + W
    d["NIC"] = D // 128
    d["NOC"] = D // 128
    d["NKC"] = DKV // 128
    d["NQH"] = D // 128
    d["NKVH"] = DKV // 128
    d["G"] = d["NQH"] // d["NKVH"]
    d["NQT"] = d["T"] // 128
    d["WKT"] = W // 128
    d["KSPAN"] = W + 128
    d["NKT"] = d["KV"] // 128
    d["NXT"] = d["KV"] // TT
    d["TSUB"] = TT // 128
    d["QXT0"] = W // TT
    assert W % TT == 0 and d["T"] % TT == 0 and TT % 128 == 0 and d["NIC"] % 4 == 0
    return d


def _mask_layout(c):
    WKT, NQT = c["WKT"], c["NQT"]
    off, plan = 0, []
    for qt in range(NQT):
        nfar = max(1, WKT - qt)
        plan.append((nfar, off, off + nfar * 128))
        off += (nfar + 1) * 128
    return off, plan


def build_mask_values(c, T0, mask_cols, plan):
    T, W, WKT, NQT = c["T"], c["W"], c["WKT"], c["NQT"]
    m = np.zeros((128, mask_cols), np.float32)
    di = np.arange(128)[:, None]
    dj = np.arange(128)[None, :]

    def blk(qt, k):
        i = T0 + 128 * qt + di
        j = T0 - W + 128 * (qt + k) + dj
        allowed = (j <= i) & (j > i - W) & (j >= 0)
        return allowed

    for qt in range(NQT):
        nfar, far_off, diag_off = plan[qt]
        for k in range(nfar):
            m[:, far_off + 128 * k: far_off + 128 * (k + 1)] = np.where(blk(qt, k), 0.0, NEG)
        for k in range(nfar, WKT):
            assert blk(qt, k).all(), (T0, qt, k)
        m[:, diag_off: diag_off + 128] = np.where(blk(qt, WKT), 0.0, NEG)
    return m


def build_rope_tables(c, T0):
    T, W = c["T"], c["W"]
    sm_scale = 1.0 / math.sqrt(128.0)
    inv_freq = ROPE_THETA ** (-np.arange(64, dtype=np.float64) / 64.0)

    def tables(pos, scale):
        ang = pos[None, :] * inv_freq[:, None]
        cos = np.cos(ang)
        sin = np.sin(ang)
        ct = np.concatenate([cos, cos], 0) * scale
        st = np.concatenate([-sin, sin], 0) * scale
        return ct.astype(np.float32), st.astype(np.float32)

    cq, sq = tables(np.arange(T0, T0 + T, dtype=np.float64), sm_scale)
    ck, sk = tables(np.arange(T0 - W, T0 + T, dtype=np.float64), 1.0)
    return cq, sq, ck, sk


def build_nc(cfg):
    c = _derived(cfg)
    D, DKV, T, KV, W, R, TT = c["D"], c["DKV"], c["T"], c["KV"], c["W"], c["R"], c["TT"]
    NIC, NOC, NKC = c["NIC"], c["NOC"], c["NKC"]
    NQT, WKT, KSPAN, NKT = c["NQT"], c["WKT"], c["KSPAN"], c["NKT"]
    NXT, TSUB, QXT0, NQH, NKVH, G = c["NXT"], c["TSUB"], c["QXT0"], c["NQH"], c["NKVH"], c["G"]
    mask_cols, mask_plan = _mask_layout(c)
    NSP = (KSPAN + 511) // 512
    NTG = (WKT + 1 + 3) // 4

    nc = bass.Bass("TRN2", target_bir_lowering=False, debug=False)

    xin = nc.dram_tensor("xin", [KV, D], F32, kind="ExternalInput").ap()
    wq_d = nc.dram_tensor("wq", [D, D], F32, kind="ExternalInput").ap()
    wk_d = nc.dram_tensor("wk", [D, DKV], F32, kind="ExternalInput").ap()
    wv_d = nc.dram_tensor("wv", [D, DKV], F32, kind="ExternalInput").ap()
    wo_d = nc.dram_tensor("wo", [D, D], F32, kind="ExternalInput").ap()
    wb_d = nc.dram_tensor("wb", [D, D], F32, kind="ExternalInput").ap()
    bb_d = nc.dram_tensor("bb", [NOC, 128], F32, kind="ExternalInput").ap()
    A_d = nc.dram_tensor("Ain", [R, D], F32, kind="ExternalInput").ap()
    WL_d = nc.dram_tensor("WLin", [R, D], F32, kind="ExternalInput").ap()
    B_d = nc.dram_tensor("Bin", [D, R], F32, kind="ExternalInput").ap()
    WR_d = nc.dram_tensor("WRin", [D, R], F32, kind="ExternalInput").ap()
    msk_d = nc.dram_tensor("msk", [128, mask_cols], F32, kind="ExternalInput").ap()
    cq_d = nc.dram_tensor("cq", [128, T], F32, kind="ExternalInput").ap()
    sq_d = nc.dram_tensor("sq", [128, T], F32, kind="ExternalInput").ap()
    ck_d = nc.dram_tensor("ck", [128, KV], F32, kind="ExternalInput").ap()
    sk_d = nc.dram_tensor("sk", [128, KV], F32, kind="ExternalInput").ap()
    ident_d = nc.dram_tensor("ident", [128, 128], F32, kind="ExternalInput").ap()
    rotm_d = nc.dram_tensor("rotm", [128, 128], F32, kind="ExternalInput").ap()

    out_d = nc.dram_tensor("out", [T, D], F32, kind="ExternalOutput").ap()

    BF16 = mybir.dt.bfloat16
    qhi_s = nc.dram_tensor("qhi_s", [D, T], BF16, kind="Internal").ap()
    qlo_s = nc.dram_tensor("qlo_s", [D, T], BF16, kind="Internal").ap()
    khi_s = nc.dram_tensor("khi_s", [DKV, KV], BF16, kind="Internal").ap()
    klo_s = nc.dram_tensor("klo_s", [DKV, KV], BF16, kind="Internal").ap()
    v_s = nc.dram_tensor("v_s", [KV, DKV], F32, kind="Internal").ap()
    o_s = nc.dram_tensor("o_s", [D, T], F32, kind="Internal").ap()
    xf_s = nc.dram_tensor("xf_s", [D, T], F32, kind="Internal").ap()
    y_s = nc.dram_tensor("y_s", [D, T], F32, kind="Internal").ap()
    qh_s = nc.dram_tensor("qh_s", [NQH, 128, NIC * 128], BF16, kind="Internal").ap()
    ql_s = nc.dram_tensor("ql_s", [NQH, 128, NIC * 128], BF16, kind="Internal").ap()
    kh_s = nc.dram_tensor("kh_s", [NKC, 128, NIC * 128], BF16, kind="Internal").ap()
    kl_s = nc.dram_tensor("kl_s", [NKC, 128, NIC * 128], BF16, kind="Internal").ap()
    vh_s = nc.dram_tensor("vh_s", [NKC, 128, NIC * 128], BF16, kind="Internal").ap()
    vl_s = nc.dram_tensor("vl_s", [NKC, 128, NIC * 128], BF16, kind="Internal").ap()
    oh_s = nc.dram_tensor("oh_s", [NOC, 128, NIC * 128], BF16, kind="Internal").ap()
    ol_s = nc.dram_tensor("ol_s", [NOC, 128, NIC * 128], BF16, kind="Internal").ap()
    wbth_s = nc.dram_tensor("wbth_s", [NOC, 128, NIC * 128], BF16, kind="Internal").ap()
    wbtl_s = nc.dram_tensor("wbtl_s", [NOC, 128, NIC * 128], BF16, kind="Internal").ap()

    pb = PB(nc)
    glob = ExitStack()
    ident = glob.enter_context(nc.sbuf_tensor("g_ident", [128, 128], F32))
    rotm = glob.enter_context(nc.sbuf_tensor("g_rotm", [128, 128], F32))
    b_sb = glob.enter_context(nc.sbuf_tensor("g_bb", [128, NOC], F32))
    awlt = glob.enter_context(nc.sbuf_tensor("g_awlt", [128, NIC * R], F32))
    bwrt = glob.enter_context(nc.sbuf_tensor("g_bwrt", [R, D], F32))
    consts = glob.enter_context(nc.sbuf_tensor("g_consts", [128, 2], F32))

    t_ident = Trk()
    t_rotm = Trk()
    t_bsb = Trk()
    t_awlt = Trk()
    t_bwrt = Trk()
    t_consts = Trk()

    with ExitStack() as p0:
        awl_r = p0.enter_context(nc.sbuf_tensor("p0_awl", [R, D], F32))
        wl_r = p0.enter_context(nc.sbuf_tensor("p0_wl", [R, D], F32))
        bwr_r = p0.enter_context(nc.sbuf_tensor("p0_bwr", [128, NOC * R], F32))
        wr_r = p0.enter_context(nc.sbuf_tensor("p0_wr", [128, NOC * R], F32))
        bb_r = p0.enter_context(nc.sbuf_tensor("p0_bbr", [NOC, 128], F32))
        wbrow = [p0.enter_context(nc.sbuf_tensor(f"p0_wbr{i}", [128, D], F32)) for i in range(4)]
        wbsh = [p0.enter_context(nc.sbuf_tensor(f"p0_wbh{i}", [128, 512], BF16)) for i in range(4)]
        wbsl = [p0.enter_context(nc.sbuf_tensor(f"p0_wbl{i}", [128, 512], BF16)) for i in range(4)]
        p_a = p0.enter_context(nc.psum_tensor("p0_pa", [128, 512], F32))
        p_b = p0.enter_context(nc.psum_tensor("p0_pb", [128, 512], F32))
        p_c = p0.enter_context(nc.psum_tensor("p0_pc", [128, 512], F32))
        p_d = p0.enter_context(nc.psum_tensor("p0_pd", [128, 512], F32))

        ld_ident = pb.add("sync", lambda e: e.dma_start(out=ident[:], in_=ident_d[:, :]), dma=True)
        t_ident.did_wr(ld_ident)
        ld_rot = pb.add("sync", lambda e: e.dma_start(out=rotm[:], in_=rotm_d[:, :]), dma=True)
        t_rotm.did_wr(ld_rot)
        cset0 = pb.add("vector", lambda e: e.memset(consts[:, 0:1], EXP_SHIFT))
        cset1 = pb.add("vector", lambda e: e.memset(consts[:, 1:2], LN_EPS), deps=[cset0])
        t_consts.did_wr(cset1)
        ld_bb = pb.add("sync", lambda e: e.dma_start(out=bb_r[:NOC, :], in_=bb_d[:, :]), dma=True)
        ld_a = pb.add("sync", lambda e: e.dma_start(out=awl_r[:R, :], in_=A_d[:, :]), dma=True)
        ld_wl = pb.add("sync", lambda e: e.dma_start(out=wl_r[:R, :], in_=WL_d[:, :]), dma=True)
        ld_b = pb.add("sync", lambda e: e.dma_start(
            out=bwr_r.ap().rearrange("p (c r) -> p c r", r=R),
            in_=B_d.rearrange("(c p) r -> p c r", p=128)), dma=True)
        ld_wr = pb.add("sync", lambda e: e.dma_start(
            out=wr_r.ap().rearrange("p (c r) -> p c r", r=R),
            in_=WR_d.rearrange("(c p) r -> p c r", p=128)), dma=True)

        add_awl = pb.add("vector", lambda e: e.tensor_tensor(
            awl_r[:R, :], awl_r[:R, :], wl_r[:R, :], op=ALU.add), deps=[ld_a, ld_wl])
        add_bwr = pb.add("vector", lambda e: e.tensor_tensor(
            bwr_r[:], bwr_r[:], wr_r[:], op=ALU.add), deps=[ld_b, ld_wr])

        t_pa = Trk()
        t_pb = Trk()
        tb = pb.add("tensor", lambda e: e.transpose(p_a[:, 0:NOC], bb_r[:NOC, :], ident[0:NOC, 0:NOC]),
                    deps=[ld_bb, ld_ident] + t_pa.wr())
        t_pa.did_wr(tb)
        ev = pb.add("vector", lambda e: e.tensor_copy(b_sb[:, 0:NOC], p_a[:, 0:NOC]), deps=[tb])
        t_pa.did_rd(ev)
        t_bsb.did_wr(ev)
        t_pc = Trk()
        t_pd = Trk()
        for ic in range(NIC):
            pp, tpp = (p_c, t_pc) if ic % 2 == 0 else (p_d, t_pd)
            tr = pb.add("tensor", lambda e, ic=ic, pp=pp: e.transpose(
                pp[:, 0:R], awl_r[:R, ic * 128:(ic + 1) * 128], ident[0:R, 0:R]),
                deps=[add_awl, ld_ident] + tpp.wr())
            tpp.did_wr(tr)
            ev = pb.add("vector", lambda e, ic=ic, pp=pp: e.tensor_copy(
                awlt[:, ic * R:(ic + 1) * R], pp[:, 0:R]), deps=[tr])
            tpp.did_rd(ev)
        t_awlt.did_wr(ev)
        for oc in range(NOC):
            pp, tpp = (p_c, t_pc) if oc % 2 == 0 else (p_d, t_pd)
            tr = pb.add("tensor", lambda e, oc=oc, pp=pp: e.transpose(
                pp[:R, 0:128], bwr_r[:, oc * R:(oc + 1) * R], ident[:, :]),
                deps=[add_bwr, ld_ident] + tpp.wr())
            tpp.did_wr(tr)
            ev = pb.add("vector", lambda e, oc=oc, pp=pp: e.tensor_copy(
                bwrt[:R, oc * 128:(oc + 1) * 128], pp[:R, 0:128]), deps=[tr])
            tpp.did_rd(ev)
        t_bwrt.did_wr(ev)

        t_row = Trk(4)
        t_stg = Trk(4)
        NB4 = NIC // 4
        for oc in range(NOC):
            ld = pb.add("sync", lambda e, oc=oc: e.dma_start(
                out=wbrow[oc % 4][:], in_=wb_d[oc * 128:(oc + 1) * 128, :]),
                deps=t_row.wr(oc % 4), dma=True)
            t_row.did_wr(ld, oc % 4)
            for g in range(NB4):
                pt = p_a if g % 2 == 0 else p_b
                tp = t_pa if g % 2 == 0 else t_pb
                trs = []
                for t4 in range(4):
                    ic = g * 4 + t4
                    tr = pb.add("tensor", lambda e, oc=oc, ic=ic, t4=t4, pt=pt: e.transpose(
                        pt[:, t4 * 128:(t4 + 1) * 128],
                        wbrow[oc % 4][:, ic * 128:(ic + 1) * 128], ident[:, :]),
                        deps=[ld, ld_ident] + (tp.wr() if t4 == 0 else []))
                    trs.append(tr)
                    t_row.did_rd(tr, oc % 4)
                tp.did_wr(trs[-1])
                for tr in trs[:-1]:
                    tp.r[0].append(tr)
                sl = (oc * NB4 + g) % 4
                evh = pb.add("scalar", lambda e, sl=sl, pt=pt: e.activation(
                    wbsh[sl][:], pt[:], AF.Copy, bias=0.0, scale=1.0),
                    deps=trs + t_stg.wr(sl))
                tp.did_rd(evh)
                evl = pb.add("vector", lambda e, sl=sl, pt=pt: e.tensor_tensor(
                    wbsl[sl][:], pt[:], wbsh[sl][:], op=ALU.subtract),
                    deps=trs + [evh] + t_stg.wr(sl))
                tp.did_rd(evl)
                t_stg.did_wr(evl, sl)
                t_stg.r[sl].append(evh)
                sth = pb.add("sync", lambda e, oc=oc, g=g, sl=sl: e.dma_start(
                    out=wbth_s[oc, :, g * 512:(g + 1) * 512], in_=wbsh[sl][:]),
                    deps=[evh, evl], dma=True)
                stl = pb.add("sync", lambda e, oc=oc, g=g, sl=sl: e.dma_start(
                    out=wbtl_s[oc, :, g * 512:(g + 1) * 512], in_=wbsl[sl][:]),
                    deps=[evl], dma=True)
                t_stg.did_rd(sth, sl)
                t_stg.did_rd(stl, sl)
        pb.barrier()

    with ExitStack() as p1:
        xtm = [p1.enter_context(nc.sbuf_tensor(f"p1_xtm{i}", [128, D], F32)) for i in range(2)]
        xfh = p1.enter_context(nc.sbuf_tensor("p1_xfh", [128, NIC * TT], BF16))
        xfl = p1.enter_context(nc.sbuf_tensor("p1_xfl", [128, NIC * TT], BF16))
        xst = [p1.enter_context(nc.sbuf_tensor(f"p1_xst{i}", [128, 512], F32)) for i in range(2)]
        wfh = [p1.enter_context(nc.sbuf_tensor(f"p1_wh{i}", [128, NIC * 128], BF16)) for i in range(2)]
        wfl = [p1.enter_context(nc.sbuf_tensor(f"p1_wl{i}", [128, NIC * 128], BF16)) for i in range(2)]
        wf32 = p1.enter_context(nc.sbuf_tensor("p1_w32", [128, NIC * 128], F32))
        qsb = [p1.enter_context(nc.sbuf_tensor(f"p1_q{i}", [128, TT], F32)) for i in range(2)]
        qout = [p1.enter_context(nc.sbuf_tensor(f"p1_qo{i}", [128, TT], F32)) for i in range(2)]
        qhb = [p1.enter_context(nc.sbuf_tensor(f"p1_qh{i}", [128, TT], BF16)) for i in range(2)]
        qlb = [p1.enter_context(nc.sbuf_tensor(f"p1_ql{i}", [128, TT], BF16)) for i in range(2)]
        t1b = p1.enter_context(nc.sbuf_tensor("p1_t1", [128, TT], F32))
        t2b = p1.enter_context(nc.sbuf_tensor("p1_t2", [128, TT], F32))
        vstg = [p1.enter_context(nc.sbuf_tensor(f"p1_vs{i}", [128, 256], F32)) for i in range(2)]
        rtq = p1.enter_context(nc.sbuf_tensor("p1_rtq", [128, 2 * TT], F32))
        rtk = p1.enter_context(nc.sbuf_tensor("p1_rtk", [128, 2 * TT], F32))
        p_x = [p1.enter_context(nc.psum_tensor(f"p1_px{i}", [128, 512], F32)) for i in range(2)]
        p_q = [p1.enter_context(nc.psum_tensor(f"p1_pq{i}", [128, TT], F32)) for i in range(2)]
        p_r = [p1.enter_context(nc.psum_tensor(f"p1_pr{i}", [128, TT], F32)) for i in range(2)]
        p_v = [p1.enter_context(nc.psum_tensor(f"p1_pv{i}", [128, 256], F32)) for i in range(2)]

        t_xtm = Trk(2)
        t_xfm = Trk()
        t_xst = Trk(2)
        t_wb = Trk(2)
        t_w32 = Trk()
        t_qsb = Trk(2)
        t_qout = Trk(2)
        t_qhb = Trk(2)
        t_qlb = Trk(2)
        t_t1 = Trk()
        t_t2 = Trk()
        t_vstg = Trk(2)
        t_rtq = Trk()
        t_rtk = Trk()
        t_px = [Trk() for _ in range(2)]
        t_pq = [Trk() for _ in range(2)]
        t_pr = [Trk() for _ in range(2)]
        t_pv = [Trk() for _ in range(2)]
        t_ws = {"q": Trk(), "k": Trk(), "v": Trk()}

        wsc = {"q": (wq_d, qh_s, ql_s), "k": (wk_d, kh_s, kl_s), "v": (wv_d, vh_s, vl_s)}

        wi = 0
        for xt in range(NXT):
            t0 = xt * TT
            qr = xt >= QXT0
            tq0 = t0 - W
            xfm_wr = []
            for sub in range(TSUB):
                par = (xt * TSUB + sub) % 2
                ld = pb.add("sync", lambda e, t0=t0, sub=sub, par=par: e.dma_start(
                    out=xtm[par][:], in_=xin[t0 + sub * 128: t0 + (sub + 1) * 128, :]),
                    deps=t_xtm.wr(par), dma=True)
                t_xtm.did_wr(ld, par)
                for g in range(NIC // 4):
                    gp = g % 2
                    trs = []
                    for t4 in range(4):
                        ic = g * 4 + t4
                        tr = pb.add("tensor", lambda e, par=par, ic=ic, t4=t4, gp=gp: e.transpose(
                            p_x[gp][:, t4 * 128:(t4 + 1) * 128],
                            xtm[par][:, ic * 128:(ic + 1) * 128], ident[:, :]),
                            deps=[ld] + (t_px[gp].wr() if t4 == 0 else []))
                        t_ident.did_rd(tr)
                        t_xtm.did_rd(tr, par)
                        trs.append(tr)
                    t_px[gp].did_wr(trs[-1])
                    t_px[gp].r[0].extend(trs[:-1])
                    xv = lambda t_: t_.ap().rearrange("p (ic t) -> p ic t", t=TT)
                    evh = pb.add("scalar", lambda e, g=g, sub=sub, gp=gp, xv=xv: e.activation(
                        xv(xfh)[:, g * 4:(g + 1) * 4, sub * 128:(sub + 1) * 128],
                        p_x[gp][:], AF.Copy, bias=0.0, scale=1.0),
                        deps=trs + (t_xfm.wr() if (sub == 0 and g == 0) else []))
                    t_px[gp].did_rd(evh)
                    evl = pb.add("vector", lambda e, g=g, sub=sub, gp=gp, xv=xv: e.tensor_tensor(
                        xv(xfl)[:, g * 4:(g + 1) * 4, sub * 128:(sub + 1) * 128],
                        p_x[gp][:],
                        xv(xfh)[:, g * 4:(g + 1) * 4, sub * 128:(sub + 1) * 128],
                        op=ALU.subtract),
                        deps=trs + [evh])
                    t_px[gp].did_rd(evl)
                    xfm_wr.extend([evh, evl])
                    if qr:
                        ev32 = pb.add("vector", lambda e, gp=gp: e.tensor_copy(
                            xst[gp][:], p_x[gp][:]), deps=trs + t_xst.wr(gp))
                        t_px[gp].did_rd(ev32)
                        t_xst.did_wr(ev32, gp)
                        st32 = pb.add("gpsimd", lambda e, g=g, sub=sub, tq0=tq0, gp=gp: e.dma_start(
                            out=xf_s.rearrange("(ic p) t -> p ic t", p=128)
                            [:, g * 4:(g + 1) * 4, tq0 + sub * 128: tq0 + (sub + 1) * 128],
                            in_=xst[gp].ap().rearrange("p (ic t) -> p ic t", t=128)),
                            deps=[ev32], dma=True)
                        t_xst.did_rd(st32, gp)
            t_xfm.did_wr(xfm_wr[-1])
            t_xfm.r[0].extend(xfm_wr[:-1])
            lrk = pb.add("sync", lambda e, t0=t0: e.dma_start(
                out=rtk[:, 0:TT], in_=ck_d[:, t0:t0 + TT]), deps=t_rtk.wr(), dma=True)
            lrk2 = pb.add("sync", lambda e, t0=t0: e.dma_start(
                out=rtk[:, TT:2 * TT], in_=sk_d[:, t0:t0 + TT]), deps=[], dma=True)
            t_rtk.did_wr(lrk)
            t_rtk.r[0].append(lrk2)
            if qr:
                lrq = pb.add("sync", lambda e, tq0=tq0: e.dma_start(
                    out=rtq[:, 0:TT], in_=cq_d[:, tq0:tq0 + TT]), deps=t_rtq.wr(), dma=True)
                lrq2 = pb.add("sync", lambda e, tq0=tq0: e.dma_start(
                    out=rtq[:, TT:2 * TT], in_=sq_d[:, tq0:tq0 + TT]), deps=[], dma=True)
                t_rtq.did_wr(lrq)
                t_rtq.r[0].append(lrq2)

            steps = []
            if qr:
                steps += [("q", oc) for oc in range(NQH)]
            steps += [("k", oc) for oc in range(NKC)]
            steps += [("v", oc) for oc in range(NKC)]
            for kind, oc in steps:
                wp = wi % 2
                wraw, whs, wls = wsc[kind]
                first_xt = QXT0 if kind == "q" else 0
                cw = 128
                if xt == first_xt:
                    ldw32 = pb.add("sync", lambda e, wraw=wraw, oc=oc, cw=cw: e.dma_start(
                        out=wf32.ap().rearrange("p (ic n) -> p ic n", n=cw)[:, :, 0:cw] if cw != 128
                        else wf32.ap().rearrange("p (ic n) -> p ic n", n=128),
                        in_=wraw.rearrange("(ic p) n -> p ic n", p=128)[:, :, oc * cw:(oc + 1) * cw]),
                        deps=t_w32.wr(), dma=True)
                    t_w32.did_wr(ldw32)
                    qs_ops = []
                    QW = NIC * cw // 4
                    for qi in range(4):
                        sph = pb.add("scalar", lambda e, wp=wp, qi=qi, QW=QW: e.activation(
                            wfh[wp][:, qi * QW:(qi + 1) * QW], wf32[:, qi * QW:(qi + 1) * QW],
                            AF.Copy, bias=0.0, scale=1.0),
                            deps=[ldw32] + (t_wb.wr(wp) if qi == 0 else []))
                        t_w32.did_rd(sph)
                        spl = pb.add("vector", lambda e, wp=wp, qi=qi, QW=QW: e.tensor_tensor(
                            wfl[wp][:, qi * QW:(qi + 1) * QW], wf32[:, qi * QW:(qi + 1) * QW],
                            wfh[wp][:, qi * QW:(qi + 1) * QW], op=ALU.subtract),
                            deps=[ldw32, sph] + (t_wb.wr(wp) if qi == 0 else []))
                        t_w32.did_rd(spl)
                        qs_ops.extend([sph, spl])
                    t_wb.did_wr(qs_ops[-1], wp)
                    t_wb.r[wp].extend(qs_ops[:-1])
                    ldw = qs_ops[-1]
                    ldwh = qs_ops[-2]
                    sth = pb.add("gpsimd", lambda e, whs=whs, oc=oc, wp=wp: e.dma_start(
                        out=whs[oc], in_=wfh[wp][:]), deps=[sph, spl], dma=True)
                    stl = pb.add("gpsimd", lambda e, wls=wls, oc=oc, wp=wp: e.dma_start(
                        out=wls[oc], in_=wfl[wp][:]), deps=[spl], dma=True)
                    t_wb.did_rd(sth, wp)
                    t_wb.did_rd(stl, wp)
                    t_ws[kind].did_wr(stl)
                    t_ws[kind].r[0].append(sth)
                else:
                    ldwh = pb.add("sync", lambda e, whs=whs, oc=oc, wp=wp: e.dma_start(
                        out=wfh[wp][:], in_=whs[oc]),
                        deps=t_wb.wr(wp) + t_ws[kind].rd(), dma=True)
                    ldw = pb.add("sync", lambda e, wls=wls, oc=oc, wp=wp: e.dma_start(
                        out=wfl[wp][:], in_=wls[oc]),
                        deps=t_wb.wr(wp) + t_ws[kind].rd(), dma=True)
                    t_ws[kind].did_rd(ldwh)
                    t_ws[kind].did_rd(ldw)
                    t_wb.did_wr(ldw, wp)
                    t_wb.r[wp].append(ldwh)
                wi += 1
                if kind in ("q", "k"):
                    pq = wi % 2
                    NHALF = 1
                    HW_ = TT // NHALF
                    mms = []
                    for ic in range(NIC):
                        for ps_, wb_, xb_ in ((0, wfh, xfh), (1, wfh, xfl), (2, wfl, xfh)):
                            for hf in range(NHALF):
                                mm = pb.add("tensor", lambda e, ic=ic, pq=pq, wp=wp, wb_=wb_, xb_=xb_, ps_=ps_, hf=hf: e.matmul(
                                    p_q[pq][:, hf * HW_:(hf + 1) * HW_],
                                    wb_[wp][:, ic * 128:(ic + 1) * 128],
                                    xb_[:, ic * TT + hf * HW_: ic * TT + (hf + 1) * HW_],
                                    start=(ic == 0 and ps_ == 0 and hf == 0),
                                    stop=(ic == NIC - 1 and ps_ == 2 and hf == NHALF - 1)),
                                    deps=[ldw, ldwh] + xfm_wr + (t_pq[pq].wr() if (ic == 0 and ps_ == 0 and hf == 0) else []))
                                t_wb.did_rd(mm, wp)
                                t_xfm.did_rd(mm)
                                mms.append(mm)
                    t_pq[pq].did_wr(mms[-1])
                    t_pq[pq].r[0].extend(mms[:-1])
                    evq = pb.add("vector", lambda e, pq=pq: e.tensor_copy(qsb[pq][:], p_q[pq][:]),
                                 deps=mms + t_qsb.wr(pq))
                    t_pq[pq].did_rd(evq)
                    t_qsb.did_wr(evq, pq)
                    rot = pb.add("tensor", lambda e, pq=pq: e.matmul(
                        p_r[pq][:], rotm[:], qsb[pq][:], start=True, stop=True),
                        deps=[evq, ld_rot] + t_pr[pq].wr())
                    t_qsb.did_rd(rot, pq)
                    t_pr[pq].did_wr(rot)
                    rt_ = rtq if kind == "q" else rtk
                    trt = t_rtq if kind == "q" else t_rtk
                    m1 = pb.add("vector", lambda e, pq=pq, rt_=rt_: e.tensor_tensor(
                        t1b[:], qsb[pq][:], rt_[:, 0:TT], op=ALU.mult),
                        deps=[evq, (lrq if kind == "q" else lrk)] + t_t1.wr())
                    t_qsb.did_rd(m1, pq)
                    trt.did_rd(m1)
                    t_t1.did_wr(m1)
                    m2 = pb.add("vector", lambda e, pq=pq, rt_=rt_: e.tensor_tensor(
                        t2b[:], p_r[pq][:], rt_[:, TT:2 * TT], op=ALU.mult),
                        deps=[rot, (lrq2 if kind == "q" else lrk2)] + t_t2.wr())
                    t_pr[pq].did_rd(m2)
                    trt.did_rd(m2)
                    t_t2.did_wr(m2)
                    cmb = pb.add("vector", lambda e, pq=pq: e.tensor_tensor(
                        qout[pq][:], t1b[:], t2b[:], op=ALU.add),
                        deps=[m1, m2] + t_qout.wr(pq))
                    t_t1.did_rd(cmb)
                    t_t2.did_rd(cmb)
                    t_qout.did_wr(cmb, pq)
                    dsth = qhi_s if kind == "q" else khi_s
                    dstl = qlo_s if kind == "q" else klo_s
                    dc0 = tq0 if kind == "q" else t0
                    qsh = pb.add("scalar", lambda e, pq=pq: e.activation(
                        qhb[pq][:], qout[pq][:], AF.Copy, bias=0.0, scale=1.0),
                        deps=[cmb] + t_qhb.wr(pq))
                    t_qout.did_rd(qsh, pq)
                    t_qhb.did_wr(qsh, pq)
                    qsl = pb.add("vector", lambda e, pq=pq: e.tensor_tensor(
                        qlb[pq][:], qout[pq][:], qhb[pq][:], op=ALU.subtract),
                        deps=[cmb, qsh] + t_qlb.wr(pq))
                    t_qout.did_rd(qsl, pq)
                    t_qhb.did_rd(qsl, pq)
                    t_qlb.did_wr(qsl, pq)
                    sth = pb.add("gpsimd", lambda e, dsth=dsth, oc=oc, dc0=dc0, pq=pq: e.dma_start(
                        out=dsth[oc * 128:(oc + 1) * 128, dc0:dc0 + TT], in_=qhb[pq][:]),
                        deps=[qsh], dma=True)
                    t_qhb.did_rd(sth, pq)
                    stl = pb.add("gpsimd", lambda e, dstl=dstl, oc=oc, dc0=dc0, pq=pq: e.dma_start(
                        out=dstl[oc * 128:(oc + 1) * 128, dc0:dc0 + TT], in_=qlb[pq][:]),
                        deps=[qsl], dma=True)
                    t_qlb.did_rd(stl, pq)
                else:
                    pq = wi % 2
                    NHALF = 1
                    HW_ = TT // NHALF
                    mms = []
                    for ic in range(NIC):
                        for ps_, wb_, xb_ in ((0, wfh, xfh), (1, wfh, xfl), (2, wfl, xfh)):
                            for hf in range(NHALF):
                                mm = pb.add("tensor", lambda e, ic=ic, pq=pq, wp=wp, wb_=wb_, xb_=xb_, ps_=ps_, hf=hf: e.matmul(
                                    p_q[pq][:, hf * HW_:(hf + 1) * HW_],
                                    wb_[wp][:, ic * 128:(ic + 1) * 128],
                                    xb_[:, ic * TT + hf * HW_: ic * TT + (hf + 1) * HW_],
                                    start=(ic == 0 and ps_ == 0 and hf == 0),
                                    stop=(ic == NIC - 1 and ps_ == 2 and hf == NHALF - 1)),
                                    deps=[ldw, ldwh] + xfm_wr + (t_pq[pq].wr() if (ic == 0 and ps_ == 0 and hf == 0) else []))
                                t_wb.did_rd(mm, wp)
                                t_xfm.did_rd(mm)
                                mms.append(mm)
                    t_pq[pq].did_wr(mms[-1])
                    t_pq[pq].r[0].extend(mms[:-1])
                    evv = pb.add("vector", lambda e, pq=pq: e.tensor_copy(qsb[pq][:], p_q[pq][:]),
                                 deps=mms + t_qsb.wr(pq))
                    t_pq[pq].did_rd(evv)
                    t_qsb.did_wr(evv, pq)
                    for sub in range(TSUB):
                        pv = sub % 2
                        tr = pb.add("tensor", lambda e, pq=pq, sub=sub, pv=pv: e.transpose(
                            p_v[pv][:, 0:128], qsb[pq][:, sub * 128:(sub + 1) * 128], ident[:, :]),
                            deps=[evv] + t_pv[pv].wr())
                        t_qsb.did_rd(tr, pq)
                        t_pv[pv].did_wr(tr)
                        ev = pb.add("vector", lambda e, pv=pv: e.tensor_copy(
                            vstg[pv][:, 0:128], p_v[pv][:, 0:128]),
                            deps=[tr] + t_vstg.wr(pv))
                        t_pv[pv].did_rd(ev)
                        t_vstg.did_wr(ev, pv)
                        st = pb.add("gpsimd", lambda e, t0=t0, sub=sub, oc=oc, pv=pv: e.dma_start(
                            out=v_s[t0 + sub * 128: t0 + (sub + 1) * 128, oc * 128:(oc + 1) * 128],
                            in_=vstg[pv][:, 0:128]), deps=[ev], dma=True)
                        t_vstg.did_rd(st, pv)
        pb.barrier()

    with ExitStack() as p2:
        mskb = p2.enter_context(nc.sbuf_tensor("p2_msk", [128, mask_cols], F32))
        khb = p2.enter_context(nc.sbuf_tensor("p2_kh", [128, KV], BF16))
        klb = p2.enter_context(nc.sbuf_tensor("p2_kl", [128, KV], BF16))
        vh = p2.enter_context(nc.sbuf_tensor("p2_vh", [128, NKT * 128], F32))
        qhhb = p2.enter_context(nc.sbuf_tensor("p2_qhh", [128, G * T], BF16))
        qhlb = p2.enter_context(nc.sbuf_tensor("p2_qhl", [128, G * T], BF16))
        usb = [p2.enter_context(nc.sbuf_tensor(f"p2_u{i}", [128, KSPAN], F32)) for i in range(2)]
        zsb = [p2.enter_context(nc.sbuf_tensor(f"p2_z{i}", [128, 1], F32)) for i in range(2)]
        zrb = [p2.enter_context(nc.sbuf_tensor(f"p2_zr{i}", [128, 1], F32)) for i in range(2)]
        ut4 = [p2.enter_context(nc.sbuf_tensor(f"p2_ut{i}", [128, (WKT + 1) * G * 128], F32)) for i in range(2)]
        ostg = [p2.enter_context(nc.sbuf_tensor(f"p2_o{i}", [128, G * 128], F32)) for i in range(2)]
        p_s = p2.enter_context(nc.psum_tensor("p2_ps", [128, 1536], F32))
        p_t = [p2.enter_context(nc.psum_tensor(f"p2_pt{i}", [128, 256], F32)) for i in range(2)]
        p_o = [p2.enter_context(nc.psum_tensor(f"p2_po{i}", [128, G * 128], F32)) for i in range(2)]

        t_msk = Trk()
        t_kh = Trk()
        t_vh = Trk()
        t_qh = Trk()
        t_u = Trk(2)
        t_z = Trk(2)
        t_zr = Trk(2)
        t_ut = Trk(2)
        t_os = Trk(2)
        t_ps = Trk()
        t_pt = Trk(2)
        t_po = [Trk() for _ in range(2)]

        ldm = pb.add("sync", lambda e: e.dma_start(out=mskb[:], in_=msk_d[:, :]), dma=True)
        t_msk.did_wr(ldm)

        GW = G * 128

        def p2_scores_exp(qt, g, up, ldq, ldk):
            nfar, far_off, diag_off = mask_plan[qt]
            mms = []
            for sp in range(NSP):
                c0 = sp * 512
                cw = min(512, KSPAN - c0)
                for pi, (qb, kb) in enumerate(((qhhb, khb), (qhhb, klb), (qhlb, khb))):
                    mm = pb.add("tensor", lambda e, g=g, qt=qt, c0=c0, cw=cw, up=up, qb=qb, kb=kb, pi=pi: e.matmul(
                        p_s[:, c0:c0 + cw],
                        qb[:, g * T + qt * 128: g * T + (qt + 1) * 128],
                        kb[:, qt * 128 + c0: qt * 128 + c0 + cw],
                        start=(pi == 0), stop=(pi == 2)),
                        deps=ldq + ldk + (t_ps.wr() if (sp == 0 and pi == 0) else []))
                    t_qh.did_rd(mm)
                    t_kh.did_rd(mm)
                    mms.append(mm)
            t_ps.did_wr(mms[-1])
            t_ps.r[0].extend(mms[:-1])
            mk1 = pb.add("vector", lambda e, nfar=nfar, far_off=far_off: e.tensor_tensor(
                p_s[:, 0:nfar * 128], p_s[:, 0:nfar * 128],
                mskb[:, far_off:far_off + nfar * 128], op=ALU.add),
                deps=mms + [ldm])
            mk2 = pb.add("vector", lambda e, diag_off=diag_off: e.tensor_tensor(
                p_s[:, WKT * 128:(WKT + 1) * 128], p_s[:, WKT * 128:(WKT + 1) * 128],
                mskb[:, diag_off:diag_off + 128], op=ALU.add),
                deps=mms + [ldm])
            t_msk.did_rd(mk1)
            t_msk.did_rd(mk2)
            t_ps.did_wr(mk2)
            t_ps.r[0].append(mk1)
            ex = pb.add("scalar", lambda e, up=up: e.activation(
                usb[up][:], p_s[:, 0:KSPAN], AF.Exp,
                bias=consts[:, 0:1], scale=1.0, accum_out=zsb[up][:]),
                deps=[mk1, mk2] + t_u.wr(up) + t_z.wr(up) + t_consts.rd())
            t_ps.did_rd(ex)
            t_consts.did_rd(ex)
            t_u.did_wr(ex, up)
            t_z.did_wr(ex, up)
            rc = pb.add("vector", lambda e, up=up: e.reciprocal(zrb[up][:], zsb[up][:]),
                        deps=[ex] + t_zr.wr(up))
            t_z.did_rd(rc, up)
            t_zr.did_wr(rc, up)
            un = pb.add("vector", lambda e, up=up: e.tensor_scalar(
                usb[up][:], usb[up][:], zrb[up][:, 0:1], None, op0=ALU.mult),
                deps=[ex, rc])
            t_zr.did_rd(un, up)
            t_u.did_wr(un, up)
            return un

        def p2_transp(g, up, jp, un, first_of_block):
            ut_wr = []
            for tg in range((WKT + 1 + 1) // 2):
                hp = tg % 2
                kts = [kt for kt in (2 * tg, 2 * tg + 1) if kt < WKT + 1]
                trs = []
                for j, kt in enumerate(kts):
                    tr = pb.add("tensor", lambda e, kt=kt, j=j, hp=hp, up=up: e.transpose(
                        p_t[hp][:, j * 128:(j + 1) * 128],
                        usb[up][:, kt * 128:(kt + 1) * 128], ident[:, :]),
                        deps=[un] + (t_pt.wr(hp) if j == 0 else []))
                    t_u.did_rd(tr, up)
                    t_ident.did_rd(tr)
                    trs.append(tr)
                t_pt.did_wr(trs[-1], hp)
                t_pt.r[hp].extend(trs[:-1])
                k0, nk = kts[0], len(kts)
                wrdep = (t_ut.wr(jp) if (first_of_block and tg == 0) else [])
                if tg % 2 == 0:
                    ev = pb.add("vector", lambda e, k0=k0, nk=nk, hp=hp, jp=jp, g=g: e.tensor_copy(
                        ut4[jp].ap().rearrange("p (kt c) -> p kt c", c=GW)
                        [:, k0:k0 + nk, g * 128:(g + 1) * 128],
                        p_t[hp][:, 0:nk * 128].rearrange("p (kt c) -> p kt c", c=128)),
                        deps=trs + wrdep)
                else:
                    ev = pb.add("scalar", lambda e, k0=k0, nk=nk, hp=hp, jp=jp, g=g: e.activation(
                        ut4[jp].ap().rearrange("p (kt c) -> p kt c", c=GW)
                        [:, k0:k0 + nk, g * 128:(g + 1) * 128],
                        p_t[hp][:, 0:nk * 128].rearrange("p (kt c) -> p kt c", c=128),
                        AF.Copy, bias=0.0, scale=1.0),
                        deps=trs + wrdep)
                t_pt.did_rd(ev, hp)
                ut_wr.append(ev)
            return ut_wr

        def p2_av_chunk(kvh, qt, jp, kts, ut_deps, ldv):
            avs = []
            for kt in kts:
                mm = pb.add("tensor", lambda e, kt=kt, qt=qt, jp=jp: e.matmul(
                    p_o[jp][:],
                    vh[:, (qt + kt) * 128:(qt + kt + 1) * 128],
                    ut4[jp][:, kt * GW:(kt + 1) * GW],
                    start=(kt == 0), stop=(kt == WKT)),
                    deps=ut_deps + ldv + (t_po[jp].wr() if kt == 0 else []))
                t_ut.did_rd(mm, jp)
                t_vh.did_rd(mm)
                if kt == 0:
                    t_po[jp].did_wr(mm)
                else:
                    t_po[jp].r[0].append(mm)
                avs.append(mm)
            return avs

        def p2_o_out(kvh, qt, jp, avs):
            ev = pb.add("vector", lambda e, jp=jp: e.tensor_copy(ostg[jp][:], p_o[jp][:]),
                        deps=avs + t_os.wr(jp))
            t_po[jp].did_rd(ev)
            t_os.did_wr(ev, jp)
            st = pb.add("gpsimd", lambda e, qt=qt, kvh=kvh, jp=jp: e.dma_start(
                out=o_s.rearrange("(h p) t -> p h t", p=128)
                [:, kvh * G:(kvh + 1) * G, qt * 128:(qt + 1) * 128],
                in_=ostg[jp].ap().rearrange("p (h q) -> p h q", q=128)),
                deps=[ev], dma=True)
            t_os.did_rd(st, jp)

        NKT1 = WKT + 1
        _bnds = [round(i * NKT1 / G) for i in range(G + 1)]
        AVSPLIT = [list(range(_bnds[i], _bnds[i + 1])) for i in range(G)]
        blocks = [(kvh, qt) for kvh in range(NKVH) for qt in range(NQT)]
        prev = None
        cur_kvh = -1
        ldq = ldk = ldv = None
        for bi, (kvh, qt) in enumerate(blocks):
            if kvh != cur_kvh:
                if prev is not None:
                    pkvh, pqt, pjp, put, pldv, pavs = prev
                    for ch in AVSPLIT:
                        pavs += p2_av_chunk(pkvh, pqt, pjp, ch, put, pldv)
                    p2_o_out(pkvh, pqt, pjp, pavs)
                    prev = None
                cur_kvh = kvh
                l1 = pb.add("sync", lambda e, kvh=kvh: e.dma_start(
                    out=khb[:], in_=khi_s[kvh * 128:(kvh + 1) * 128, :]), deps=t_kh.wr(), dma=True)
                l2 = pb.add("sync", lambda e, kvh=kvh: e.dma_start(
                    out=klb[:], in_=klo_s[kvh * 128:(kvh + 1) * 128, :]), deps=t_kh.wr(), dma=True)
                t_kh.did_wr(l2)
                t_kh.r[0].append(l1)
                ldk = [l1, l2]
                l3 = pb.add("sync", lambda e, kvh=kvh: e.dma_start(
                    out=vh.ap().rearrange("p (kt c) -> p kt c", c=128),
                    in_=v_s.rearrange("(kt p) d -> p kt d", p=128)[:, :, kvh * 128:(kvh + 1) * 128]),
                    deps=t_vh.wr(), dma=True)
                t_vh.did_wr(l3)
                ldv = [l3]
                l4 = pb.add("sync", lambda e, kvh=kvh: e.dma_start(
                    out=qhhb.ap().rearrange("p (h t) -> p h t", t=T),
                    in_=qhi_s.rearrange("(h p) t -> p h t", p=128)[:, kvh * G:(kvh + 1) * G, :]),
                    deps=t_qh.wr(), dma=True)
                l5 = pb.add("sync", lambda e, kvh=kvh: e.dma_start(
                    out=qhlb.ap().rearrange("p (h t) -> p h t", t=T),
                    in_=qlo_s.rearrange("(h p) t -> p h t", p=128)[:, kvh * G:(kvh + 1) * G, :]),
                    deps=t_qh.wr(), dma=True)
                t_qh.did_wr(l5)
                t_qh.r[0].append(l4)
                ldq = [l4, l5]
            jp = bi % 2
            ut_deps = []
            for g in range(G):
                un = p2_scores_exp(qt, g, g % 2, ldq, ldk)
                ut_deps += p2_transp(g, g % 2, jp, un, first_of_block=(g == 0))
                if prev is not None and g < len(AVSPLIT):
                    pkvh, pqt, pjp, put, pldv, pavs = prev
                    pavs += p2_av_chunk(pkvh, pqt, pjp, AVSPLIT[g], put, pldv)
                    prev = (pkvh, pqt, pjp, put, pldv, pavs)
                    if g == G - 1:
                        p2_o_out(pkvh, pqt, pjp, pavs)
                        prev = None
            prev = (kvh, qt, jp, ut_deps, ldv, [])
        if prev is not None:
            pkvh, pqt, pjp, put, pldv, pavs = prev
            for ch in AVSPLIT:
                pavs += p2_av_chunk(pkvh, pqt, pjp, ch, put, pldv)
            p2_o_out(pkvh, pqt, pjp, pavs)
        pb.barrier()

    TT3 = min(TT, 256)
    TSUB3 = TT3 // 128
    with ExitStack() as p3:
        ob32 = p3.enter_context(nc.sbuf_tensor("p3_ob32", [128, NIC * TT3], F32))
        ofh = p3.enter_context(nc.sbuf_tensor("p3_ofh", [128, NIC * TT3], BF16))
        ofl = p3.enter_context(nc.sbuf_tensor("p3_ofl", [128, NIC * TT3], BF16))
        w3h = [p3.enter_context(nc.sbuf_tensor(f"p3_wh{i}", [128, NIC * 128], BF16)) for i in range(2)]
        w3l = [p3.enter_context(nc.sbuf_tensor(f"p3_wl{i}", [128, NIC * 128], BF16)) for i in range(2)]
        w332 = p3.enter_context(nc.sbuf_tensor("p3_w32", [128, NIC * 128], F32))
        xres = [p3.enter_context(nc.sbuf_tensor(f"p3_x{i}", [128, TT3], F32)) for i in range(2)]
        s1fm = p3.enter_context(nc.sbuf_tensor("p3_s1", [128, NOC * TT3], F32))
        s1tm = p3.enter_context(nc.sbuf_tensor("p3_s1t", [128, D], F32))
        ytm = p3.enter_context(nc.sbuf_tensor("p3_ytm", [128, D], F32))
        ystg = [p3.enter_context(nc.sbuf_tensor(f"p3_ys{i}", [128, 512], F32)) for i in range(2)]
        bnst = p3.enter_context(nc.sbuf_tensor("p3_bn", [128, 6 * (D // 512)], F32))
        bnag = p3.enter_context(nc.sbuf_tensor("p3_bna", [128, 2], F32))
        stdb = p3.enter_context(nc.sbuf_tensor("p3_std", [128, 1], F32))
        rstd = p3.enter_context(nc.sbuf_tensor("p3_rstd", [128, 1], F32))
        p_t3 = [p3.enter_context(nc.psum_tensor(f"p3_pt{i}", [128, 512], F32)) for i in range(2)]
        p_w3 = [p3.enter_context(nc.psum_tensor(f"p3_pw{i}", [128, TT3], F32)) for i in range(2)]
        p_y3 = [p3.enter_context(nc.psum_tensor(f"p3_py{i}", [128, 512], F32)) for i in range(2)]

        t_ob = Trk()
        t_ofm = Trk()
        t_wb3 = Trk(2)
        t_w332 = Trk()
        t_wos = Trk()
        t_xr = Trk(2)
        t_s1 = Trk()
        t_s1t = Trk()
        t_ytm = Trk()
        t_ys = Trk(2)
        t_bn = Trk()
        t_bna = Trk()
        t_std = Trk()
        t_rstd = Trk()
        t_pt3 = [Trk() for _ in range(2)]
        t_pw3 = [Trk() for _ in range(2)]
        t_py3 = [Trk() for _ in range(2)]

        for ti in range(T // TT3):
            t0 = ti * TT3
            ldo = pb.add("sync", lambda e, t0=t0: e.dma_start(
                out=ob32.ap().rearrange("p (ic t) -> p ic t", t=TT3),
                in_=o_s.rearrange("(ic p) t -> p ic t", p=128)[:, :, t0:t0 + TT3]),
                deps=t_ob.wr(), dma=True)
            t_ob.did_wr(ldo)
            ofm_wr = []
            for qi in range(4):
                QW3 = NIC * TT3 // 4
                sh = pb.add("scalar", lambda e, qi=qi, QW3=QW3: e.activation(
                    ofh[:, qi * QW3:(qi + 1) * QW3], ob32[:, qi * QW3:(qi + 1) * QW3],
                    AF.Copy, bias=0.0, scale=1.0),
                    deps=[ldo] + (t_ofm.wr() if qi == 0 else []))
                t_ob.did_rd(sh)
                sl = pb.add("vector", lambda e, qi=qi, QW3=QW3: e.tensor_tensor(
                    ofl[:, qi * QW3:(qi + 1) * QW3], ob32[:, qi * QW3:(qi + 1) * QW3],
                    ofh[:, qi * QW3:(qi + 1) * QW3], op=ALU.subtract),
                    deps=[ldo, sh] + (t_ofm.wr() if qi == 0 else []))
                t_ob.did_rd(sl)
                ofm_wr.extend([sh, sl])
            t_ofm.did_wr(ofm_wr[-1])
            t_ofm.r[0].extend(ofm_wr[:-1])
            s1_wr = []
            for oc in range(NOC):
                wp = oc % 2
                if ti == 0:
                    ldw32 = pb.add("sync", lambda e, oc=oc: e.dma_start(
                        out=w332.ap().rearrange("p (ic n) -> p ic n", n=128),
                        in_=wo_d.rearrange("(ic p) n -> p ic n", p=128)[:, :, oc * 128:(oc + 1) * 128]),
                        deps=t_w332.wr(), dma=True)
                    t_w332.did_wr(ldw32)
                    sph = pb.add("scalar", lambda e, wp=wp: e.activation(
                        w3h[wp][:], w332[:], AF.Copy, bias=0.0, scale=1.0),
                        deps=[ldw32] + t_wb3.wr(wp))
                    t_w332.did_rd(sph)
                    spl = pb.add("vector", lambda e, wp=wp: e.tensor_tensor(
                        w3l[wp][:], w332[:], w3h[wp][:], op=ALU.subtract),
                        deps=[ldw32, sph] + t_wb3.wr(wp))
                    t_w332.did_rd(spl)
                    t_wb3.did_wr(spl, wp)
                    t_wb3.r[wp].append(sph)
                    ldw, ldwh = spl, sph
                    sth = pb.add("gpsimd", lambda e, oc=oc, wp=wp: e.dma_start(
                        out=oh_s[oc], in_=w3h[wp][:]), deps=[sph, spl], dma=True)
                    stl = pb.add("gpsimd", lambda e, oc=oc, wp=wp: e.dma_start(
                        out=ol_s[oc], in_=w3l[wp][:]), deps=[spl], dma=True)
                    t_wb3.did_rd(sth, wp)
                    t_wb3.did_rd(stl, wp)
                    t_wos.did_wr(stl)
                    t_wos.r[0].append(sth)
                else:
                    ldwh = pb.add("sync", lambda e, oc=oc, wp=wp: e.dma_start(
                        out=w3h[wp][:], in_=oh_s[oc]),
                        deps=t_wb3.wr(wp) + t_wos.rd(), dma=True)
                    ldw = pb.add("sync", lambda e, oc=oc, wp=wp: e.dma_start(
                        out=w3l[wp][:], in_=ol_s[oc]),
                        deps=t_wb3.wr(wp) + t_wos.rd(), dma=True)
                    t_wos.did_rd(ldwh)
                    t_wos.did_rd(ldw)
                    t_wb3.did_wr(ldw, wp)
                    t_wb3.r[wp].append(ldwh)
                ldx = pb.add("sync", lambda e, oc=oc, t0=t0, wp=wp: e.dma_start(
                    out=xres[wp][:], in_=xf_s[oc * 128:(oc + 1) * 128, t0:t0 + TT3]),
                    deps=t_xr.wr(wp), dma=True)
                t_xr.did_wr(ldx, wp)
                pp = oc % 2
                mms = []
                for ic in range(NIC):
                    for ps_, wb_, ob_ in ((0, w3h, ofh), (1, w3h, ofl), (2, w3l, ofh)):
                        mm = pb.add("tensor", lambda e, ic=ic, pp=pp, wp=wp, wb_=wb_, ob_=ob_, ps_=ps_: e.matmul(
                            p_w3[pp][:], wb_[wp][:, ic * 128:(ic + 1) * 128],
                            ob_[:, ic * TT3:(ic + 1) * TT3],
                            start=(ic == 0 and wb_ is w3h and ob_ is ofh),
                            stop=(ic == NIC - 1 and ps_ == 2)),
                            deps=[ldw, ldwh] + ofm_wr + (t_pw3[pp].wr() if (ic == 0 and ps_ == 0) else []))
                        t_wb3.did_rd(mm, wp)
                        t_ofm.did_rd(mm)
                        mms.append(mm)
                t_pw3[pp].did_wr(mms[-1])
                t_pw3[pp].r[0].extend(mms[:-1])
                ev = pb.add("vector", lambda e, oc=oc, pp=pp, wp=wp: e.tensor_tensor(
                    s1fm[:, oc * TT3:(oc + 1) * TT3], p_w3[pp][:], xres[wp][:], op=ALU.add),
                    deps=mms + [ldx] + (t_s1.wr() if oc == 0 else []))
                t_pw3[pp].did_rd(ev)
                t_xr.did_rd(ev, wp)
                s1_wr.append(ev)
            t_s1.did_wr(s1_wr[-1])
            t_s1.r[0].extend(s1_wr[:-1])
            for sub in range(TSUB3):
                s1t_wr = []
                for g in range(NOC // 4):
                    gp = g % 2
                    trs = []
                    for t4 in range(4):
                        oc = g * 4 + t4
                        tr = pb.add("tensor", lambda e, oc=oc, sub=sub, t4=t4, gp=gp: e.transpose(
                            p_t3[gp][:, t4 * 128:(t4 + 1) * 128],
                            s1fm[:, oc * TT3 + sub * 128: oc * TT3 + (sub + 1) * 128], ident[:, :]),
                            deps=s1_wr + (t_pt3[gp].wr() if t4 == 0 else []))
                        t_s1.did_rd(tr)
                        trs.append(tr)
                    t_pt3[gp].did_wr(trs[-1])
                    t_pt3[gp].r[0].extend(trs[:-1])
                    ev = pb.add("vector", lambda e, g=g, gp=gp: e.tensor_copy(
                        s1tm[:, g * 512:(g + 1) * 512], p_t3[gp][:]),
                        deps=trs + (t_s1t.wr() if g == 0 else []))
                    t_pt3[gp].did_rd(ev)
                    s1t_wr.append(ev)
                t_s1t.did_wr(s1t_wr[-1])
                t_s1t.r[0].extend(s1t_wr[:-1])
                bns = []
                for ch in range(D // 512):
                    bn = pb.add("vector", lambda e, ch=ch: e.bn_stats(
                        bnst[:, ch * 6:(ch + 1) * 6], s1tm[:, ch * 512:(ch + 1) * 512]),
                        deps=s1t_wr + (t_bn.wr() if ch == 0 else []))
                    t_s1t.did_rd(bn)
                    bns.append(bn)
                t_bn.did_wr(bns[-1])
                t_bn.r[0].extend(bns[:-1])
                ag = pb.add("vector", lambda e: e.bn_aggr(bnag[:], bnst[:]),
                            deps=bns + t_bna.wr())
                t_bn.did_rd(ag)
                t_bna.did_wr(ag)
                sq = pb.add("scalar", lambda e: e.activation(
                    stdb[:], bnag[:, 1:2], AF.Sqrt, bias=consts[:, 1:2], scale=1.0),
                    deps=[ag] + t_std.wr() + t_consts.rd())
                t_bna.did_rd(sq)
                t_consts.did_rd(sq)
                t_std.did_wr(sq)
                rc = pb.add("vector", lambda e: e.reciprocal(rstd[:], stdb[:]),
                            deps=[sq] + t_rstd.wr())
                t_std.did_rd(rc)
                t_rstd.did_wr(rc)
                ap = pb.add("vector", lambda e: e.tensor_scalar(
                    ytm[:], s1tm[:], bnag[:, 0:1], rstd[:, 0:1],
                    op0=ALU.subtract, op1=ALU.mult),
                    deps=[ag, rc] + s1t_wr + t_ytm.wr())
                t_s1t.did_rd(ap)
                t_bna.did_rd(ap)
                t_rstd.did_rd(ap)
                t_ytm.did_wr(ap)
                for g in range(NOC // 4):
                    gp = g % 2
                    trs = []
                    for t4 in range(4):
                        oc = g * 4 + t4
                        tr = pb.add("tensor", lambda e, oc=oc, t4=t4, gp=gp: e.transpose(
                            p_y3[gp][:, t4 * 128:(t4 + 1) * 128],
                            ytm[:, oc * 128:(oc + 1) * 128], ident[:, :]),
                            deps=[ap] + (t_py3[gp].wr() if t4 == 0 else []))
                        t_ytm.did_rd(tr)
                        trs.append(tr)
                    t_py3[gp].did_wr(trs[-1])
                    t_py3[gp].r[0].extend(trs[:-1])
                    ev = pb.add("vector", lambda e, gp=gp: e.tensor_copy(ystg[gp][:], p_y3[gp][:]),
                                deps=trs + t_ys.wr(gp))
                    t_py3[gp].did_rd(ev)
                    t_ys.did_wr(ev, gp)
                    st = pb.add("gpsimd", lambda e, g=g, t0=t0, sub=sub, gp=gp: e.dma_start(
                        out=y_s.rearrange("(oc p) t -> p oc t", p=128)
                        [:, g * 4:(g + 1) * 4, t0 + sub * 128: t0 + (sub + 1) * 128],
                        in_=ystg[gp].ap().rearrange("p (oc t) -> p oc t", t=128)),
                        deps=[ev], dma=True)
                    t_ys.did_rd(st, gp)
        pb.barrier()

    TT4 = min(TT, 256)
    TSUB4 = TT4 // 128
    with ExitStack() as p4:
        yfm = p4.enter_context(nc.sbuf_tensor("p4_y", [128, NIC * TT4], F32))
        yfh = p4.enter_context(nc.sbuf_tensor("p4_yh", [128, NIC * TT4], BF16))
        yfl = p4.enter_context(nc.sbuf_tensor("p4_yl", [128, NIC * TT4], BF16))
        w4h = [p4.enter_context(nc.sbuf_tensor(f"p4_wh{i}", [128, NIC * 128], BF16)) for i in range(2)]
        w4l = [p4.enter_context(nc.sbuf_tensor(f"p4_wl{i}", [128, NIC * 128], BF16)) for i in range(2)]
        asb = p4.enter_context(nc.sbuf_tensor("p4_a", [R, TT4], F32))
        zt0 = [p4.enter_context(nc.sbuf_tensor(f"p4_z{i}", [128, TT4], F32)) for i in range(2)]
        s2fm = p4.enter_context(nc.sbuf_tensor("p4_s2", [128, NOC * TT4], F32))
        s2tm = p4.enter_context(nc.sbuf_tensor("p4_s2t", [128, D], F32))
        outtm = p4.enter_context(nc.sbuf_tensor("p4_ot", [128, D], F32))
        bnst4 = p4.enter_context(nc.sbuf_tensor("p4_bn", [128, 6 * (D // 512)], F32))
        bnag4 = p4.enter_context(nc.sbuf_tensor("p4_bna", [128, 2], F32))
        stdb4 = p4.enter_context(nc.sbuf_tensor("p4_std", [128, 1], F32))
        rstd4 = p4.enter_context(nc.sbuf_tensor("p4_rstd", [128, 1], F32))
        p_z4 = [p4.enter_context(nc.psum_tensor(f"p4_pz{i}", [128, TT4], F32)) for i in range(2)]
        p_l4 = p4.enter_context(nc.psum_tensor("p4_pl", [128, TT4], F32))
        p_a4 = p4.enter_context(nc.psum_tensor("p4_pa", [128, TT4], F32))
        p_t4 = [p4.enter_context(nc.psum_tensor(f"p4_pt{i}", [128, 512], F32)) for i in range(2)]

        t_y4 = Trk()
        t_yhl = Trk()
        t_wb4 = Trk(2)
        t_a4 = Trk()
        t_z0 = Trk(2)
        t_s2 = Trk()
        t_s2t = Trk()
        t_ot = Trk()
        t_bn4 = Trk()
        t_bna4 = Trk()
        t_std4 = Trk()
        t_rstd4 = Trk()
        t_pz4 = [Trk() for _ in range(2)]
        t_pl4 = Trk()
        t_pa4 = Trk()
        t_pt4 = [Trk() for _ in range(2)]

        out_sts = []
        for ti in range(T // TT4):
            t0 = ti * TT4
            ldy = pb.add("sync", lambda e, t0=t0: e.dma_start(
                out=yfm.ap().rearrange("p (ic t) -> p ic t", t=TT4),
                in_=y_s.rearrange("(ic p) t -> p ic t", p=128)[:, :, t0:t0 + TT4]),
                deps=t_y4.wr(), dma=True)
            t_y4.did_wr(ldy)
            yh = pb.add("scalar", lambda e: e.activation(
                yfh[:], yfm[:], AF.Copy, bias=0.0, scale=1.0),
                deps=[ldy] + t_yhl.wr())
            t_y4.did_rd(yh)
            yl = pb.add("vector", lambda e: e.tensor_tensor(
                yfl[:], yfm[:], yfh[:], op=ALU.subtract), deps=[ldy, yh])
            t_y4.did_rd(yl)
            t_yhl.did_wr(yl)
            t_yhl.r[0].append(yh)
            ams = []
            for ic in range(NIC):
                mm = pb.add("tensor", lambda e, ic=ic: e.matmul(
                    p_a4[:R, :], awlt[:, ic * R:(ic + 1) * R], yfm[:, ic * TT4:(ic + 1) * TT4],
                    start=(ic == 0), stop=(ic == NIC - 1)),
                    deps=[ldy] + t_awlt.rd() + (t_pa4.wr() if ic == 0 else []))
                t_y4.did_rd(mm)
                t_awlt.did_rd(mm)
                ams.append(mm)
            t_pa4.did_wr(ams[-1])
            t_pa4.r[0].extend(ams[:-1])
            eva = pb.add("vector", lambda e: e.tensor_copy(asb[:R, :], p_a4[:R, :]),
                         deps=ams + t_a4.wr())
            t_pa4.did_rd(eva)
            t_a4.did_wr(eva)
            s2_wr = []
            for oc in range(NOC):
                wp = oc % 2
                ldwh = pb.add("sync", lambda e, oc=oc, wp=wp: e.dma_start(
                    out=w4h[wp][:], in_=wbth_s[oc]),
                    deps=t_wb4.wr(wp), dma=True)
                ldw = pb.add("sync", lambda e, oc=oc, wp=wp: e.dma_start(
                    out=w4l[wp][:], in_=wbtl_s[oc]),
                    deps=t_wb4.wr(wp), dma=True)
                t_wb4.did_wr(ldw, wp)
                t_wb4.r[wp].append(ldwh)
                pp = oc % 2
                mms = []
                for ic in range(NIC):
                    for ps_, wb_, yb_ in ((0, w4h, yfh), (1, w4h, yfl), (2, w4l, yfh)):
                        mm = pb.add("tensor", lambda e, ic=ic, pp=pp, wp=wp, wb_=wb_, yb_=yb_, ps_=ps_: e.matmul(
                            p_z4[pp][:], wb_[wp][:, ic * 128:(ic + 1) * 128],
                            yb_[:, ic * TT4:(ic + 1) * TT4],
                            start=(ic == 0 and wb_ is w4h and yb_ is yfh),
                            stop=(ic == NIC - 1 and ps_ == 2)),
                            deps=[ldw, ldwh, yh, yl] + (t_pz4[pp].wr() if (ic == 0 and ps_ == 0) else []))
                        t_wb4.did_rd(mm, wp)
                        t_yhl.did_rd(mm)
                        mms.append(mm)
                t_pz4[pp].did_wr(mms[-1])
                t_pz4[pp].r[0].extend(mms[:-1])
                lm = pb.add("tensor", lambda e, oc=oc: e.matmul(
                    p_l4[:], bwrt[:R, oc * 128:(oc + 1) * 128], asb[:R, :],
                    start=True, stop=True),
                    deps=[eva] + t_bwrt.rd() + t_pl4.wr())
                t_a4.did_rd(lm)
                t_bwrt.did_rd(lm)
                t_pl4.did_wr(lm)
                e0 = pb.add("scalar", lambda e, oc=oc, pp=pp, wp=wp: e.activation(
                    zt0[wp][:], p_z4[pp][:], AF.Identity, bias=b_sb[:, oc:oc + 1], scale=1.0),
                    deps=mms + t_bsb.rd() + t_z0.wr(wp))
                t_pz4[pp].did_rd(e0)
                t_bsb.did_rd(e0)
                t_z0.did_wr(e0, wp)
                e1 = pb.add("vector", lambda e, oc=oc, wp=wp: e.scalar_tensor_tensor(
                    s2fm[:, oc * TT4:(oc + 1) * TT4], p_l4[:], 0.25, zt0[wp][:],
                    op0=ALU.mult, op1=ALU.add),
                    deps=[lm, e0] + (t_s2.wr() if oc == 0 else []))
                t_pl4.did_rd(e1)
                t_z0.did_rd(e1, wp)
                e2 = pb.add("vector", lambda e, oc=oc: e.tensor_tensor(
                    s2fm[:, oc * TT4:(oc + 1) * TT4], s2fm[:, oc * TT4:(oc + 1) * TT4],
                    yfm[:, oc * TT4:(oc + 1) * TT4], op=ALU.add),
                    deps=[e1])
                t_y4.did_rd(e2)
                s2_wr.append(e2)
            t_s2.did_wr(s2_wr[-1])
            t_s2.r[0].extend(s2_wr[:-1])
            for sub in range(TSUB4):
                s2t_wr = []
                for g in range(NOC // 4):
                    gp = g % 2
                    trs = []
                    for t4 in range(4):
                        oc = g * 4 + t4
                        tr = pb.add("tensor", lambda e, oc=oc, sub=sub, t4=t4, gp=gp: e.transpose(
                            p_t4[gp][:, t4 * 128:(t4 + 1) * 128],
                            s2fm[:, oc * TT4 + sub * 128: oc * TT4 + (sub + 1) * 128], ident[:, :]),
                            deps=s2_wr + (t_pt4[gp].wr() if t4 == 0 else []))
                        t_s2.did_rd(tr)
                        trs.append(tr)
                    t_pt4[gp].did_wr(trs[-1])
                    t_pt4[gp].r[0].extend(trs[:-1])
                    ev = pb.add("vector", lambda e, g=g, gp=gp: e.tensor_copy(
                        s2tm[:, g * 512:(g + 1) * 512], p_t4[gp][:]),
                        deps=trs + (t_s2t.wr() if g == 0 else []))
                    t_pt4[gp].did_rd(ev)
                    s2t_wr.append(ev)
                t_s2t.did_wr(s2t_wr[-1])
                t_s2t.r[0].extend(s2t_wr[:-1])
                bns = []
                for ch in range(D // 512):
                    bn = pb.add("vector", lambda e, ch=ch: e.bn_stats(
                        bnst4[:, ch * 6:(ch + 1) * 6], s2tm[:, ch * 512:(ch + 1) * 512]),
                        deps=s2t_wr + (t_bn4.wr() if ch == 0 else []))
                    t_s2t.did_rd(bn)
                    bns.append(bn)
                t_bn4.did_wr(bns[-1])
                t_bn4.r[0].extend(bns[:-1])
                ag = pb.add("vector", lambda e: e.bn_aggr(bnag4[:], bnst4[:]),
                            deps=bns + t_bna4.wr())
                t_bn4.did_rd(ag)
                t_bna4.did_wr(ag)
                sq = pb.add("scalar", lambda e: e.activation(
                    stdb4[:], bnag4[:, 1:2], AF.Sqrt, bias=consts[:, 1:2], scale=1.0),
                    deps=[ag] + t_std4.wr() + t_consts.rd())
                t_bna4.did_rd(sq)
                t_consts.did_rd(sq)
                t_std4.did_wr(sq)
                rc = pb.add("vector", lambda e: e.reciprocal(rstd4[:], stdb4[:]),
                            deps=[sq] + t_rstd4.wr())
                t_std4.did_rd(rc)
                t_rstd4.did_wr(rc)
                ap = pb.add("vector", lambda e: e.tensor_scalar(
                    outtm[:], s2tm[:], bnag4[:, 0:1], rstd4[:, 0:1],
                    op0=ALU.subtract, op1=ALU.mult),
                    deps=[ag, rc] + s2t_wr + t_ot.wr())
                t_s2t.did_rd(ap)
                t_bna4.did_rd(ap)
                t_rstd4.did_rd(ap)
                t_ot.did_wr(ap)
                st = pb.add("gpsimd", lambda e, t0=t0, sub=sub: e.dma_start(
                    out=out_d[t0 + sub * 128: t0 + (sub + 1) * 128, :], in_=outtm[:]),
                    deps=[ap], dma=True)
                t_ot.did_rd(st)
                out_sts.append(st)

        fin = pb.add("sync", lambda e: e.nop(), deps=out_sts)

    pb.emit()
    glob.close()
    nc.finalize()
    return nc, c, mask_cols, mask_plan


_NC_CACHE = {}


def _get_nc(cfg_key_items):
    key = tuple(sorted(cfg_key_items))
    if key not in _NC_CACHE:
        _NC_CACHE[key] = build_nc(dict(cfg_key_items))
    return _NC_CACHE[key]


def make_core_inputs(cfg, full, b, h):
    c = _derived(cfg)
    T, W, KV, D = c["T"], c["W"], c["KV"], c["D"]
    mask_cols, mask_plan = _mask_layout(c)
    T0 = h * T
    x = full["x"][b]
    if h == 0:
        xin = np.concatenate([np.zeros((W, D), np.float32), np.ascontiguousarray(x[0:T])], axis=0)
    else:
        xin = np.ascontiguousarray(x[T0 - W: T0 + T])
    cq, sq, ck, sk = build_rope_tables(c, T0)
    msk = build_mask_values(c, T0, mask_cols, mask_plan)
    rot = np.zeros((128, 128), np.float32)
    rot[np.arange(64) + 64, np.arange(64)] = 1.0
    rot[np.arange(64), np.arange(64) + 64] = 1.0
    return {
        "xin": xin,
        "wq": full["wq"], "wk": full["wk"], "wv": full["wv"], "wo": full["wo"],
        "wb": full["wb"], "bb": full["bb"].reshape(c["NOC"], 128),
        "Ain": full["A"], "WLin": full["WL"], "Bin": full["B"], "WRin": full["WR"],
        "msk": msk, "cq": cq, "sq": sq, "ck": ck, "sk": sk,
        "ident": np.eye(128, dtype=np.float32), "rotm": rot,
    }


def kernel(L0_attn_x, L0_attn_wq, L0_attn_wk, L0_attn_wv, L0_attn_wo,
           L0_ffn_W_base, L0_ffn_b_base, L0_ffn_A, L0_ffn_B, L0_ffn_WL, L0_ffn_WR):
    cfg = CFG_REAL
    c = _derived(cfg)
    nc, _, _, _ = _get_nc(tuple(cfg.items()))
    full = dict(
        x=np.ascontiguousarray(L0_attn_x, np.float32),
        wq=np.ascontiguousarray(L0_attn_wq, np.float32),
        wk=np.ascontiguousarray(L0_attn_wk, np.float32),
        wv=np.ascontiguousarray(L0_attn_wv, np.float32),
        wo=np.ascontiguousarray(L0_attn_wo, np.float32),
        wb=np.ascontiguousarray(L0_ffn_W_base, np.float32),
        bb=np.ascontiguousarray(L0_ffn_b_base, np.float32),
        A=np.ascontiguousarray(L0_ffn_A, np.float32),
        WL=np.ascontiguousarray(L0_ffn_WL, np.float32),
        B=np.ascontiguousarray(L0_ffn_B, np.float32),
        WR=np.ascontiguousarray(L0_ffn_WR, np.float32),
    )
    in_maps = []
    for core in range(8):
        b, h = core // 2, core % 2
        in_maps.append(make_core_inputs(cfg, full, b, h))
    res = run_bass_kernel_spmd(nc, in_maps, core_ids=list(range(8)))
    T, D, B = c["T"], c["D"], cfg["B"]
    outp = np.empty((B, cfg["S"], D), np.float32)
    for core in range(8):
        b, h = core // 2, core % 2
        outp[b, h * T:(h + 1) * T] = res.results[core]["out"]
    return outp


# version 14 (final)
# speedup vs baseline: 1.0395x; 1.0395x over previous
import math
import numpy as np
from contextlib import ExitStack

import concourse.bass as bass
import concourse.mybir as mybir
from concourse.bass_utils import run_bass_kernel_spmd

F32 = mybir.dt.float32
AF = mybir.ActivationFunctionType
ALU = mybir.AluOpType

NEG = -1e30
EXP_SHIFT = -12.0
LN_EPS = 1e-5
ROPE_THETA = 10000.0

CFG_REAL = dict(D=4096, DKV=1024, S=4096, B=4, W=1024, R=64, TT=512)

ENGS = ("tensor", "vector", "scalar", "gpsimd", "sync")
SEM_SEG = 48000


class _Op:
    __slots__ = ("eng", "fn", "deps", "dma", "gidx", "marked", "sem", "val", "eidx")

    def __init__(self, eng, fn, deps, dma, gidx):
        self.eng, self.fn, self.deps, self.dma, self.gidx = eng, fn, list(deps), dma, gidx
        self.marked = False
        self.sem = None
        self.val = None
        self.eidx = -1


class PB:

    def __init__(self, nc):
        self.nc = nc
        self.ops = []
        self.eng_ops = {e: [] for e in ENGS}
        self._pending_barrier = {e: [] for e in ENGS}

    def add(self, eng, fn, deps=(), dma=False):
        deps = [d for d in deps if d is not None]
        if self._pending_barrier[eng]:
            deps = deps + self._pending_barrier[eng]
            self._pending_barrier[eng] = []
        op = _Op(eng, fn, deps, dma, len(self.ops))
        self.ops.append(op)
        self.eng_ops[eng].append(op)
        op.eidx = len(self.eng_ops[eng]) - 1
        return op

    def barrier(self):
        last = []
        for e in ENGS:
            ops = self.eng_ops[e]
            comp = [op for op in ops if not op.dma]
            dmas = [op for op in ops if op.dma]
            if comp:
                last.append(comp[-1])
            last.extend(dmas[-8:])
        for e in ENGS:
            self._pending_barrier[e] = list(last)

    def emit(self):
        nc = self.nc
        for op in self.ops:
            for d in op.deps:
                if d.eng != op.eng:
                    d.marked = True
        NPOOL = 8
        self.pre_wait = {}
        for e in ENGS:
            cnt, sem, nseg = 0, None, 0
            for op in self.eng_ops[e]:
                if op.dma or not op.marked:
                    continue
                if sem is None or cnt + 1 > SEM_SEG:
                    sem = nc.alloc_semaphore(f"s_{e}_c{nseg}")
                    nseg += 1
                    cnt = 0
                cnt += 1
                op.sem, op.val = sem, cnt
            dmas = [op for op in self.eng_ops[e] if op.dma]
            if not dmas:
                continue
            pool = [nc.alloc_semaphore(f"s_{e}_d{i}") for i in range(min(NPOOL, len(dmas)))]
            pcnt = [0] * len(pool)
            for i, op in enumerate(dmas):
                k = i % len(pool)
                pcnt[k] += 16
                op.sem, op.val = pool[k], pcnt[k]
                if op.val > 16:
                    self.pre_wait[op.gidx] = (op.sem, op.val - 16)

        blk_ctx = nc.Block()
        blk = blk_ctx.__enter__()

        def section(eng_name):
            def body(eng):
                waits = {}
                last_drain = -1
                for op in self.eng_ops[eng_name]:
                    need_drain = False
                    wmap = {}
                    if op.gidx in self.pre_wait:
                        s, v = self.pre_wait[op.gidx]
                        wmap[s] = v
                    for d in op.deps:
                        if d.eng == eng_name and not d.dma:
                            if d.eidx > last_drain:
                                need_drain = True
                        else:
                            assert d.sem is not None, (d.eng, d.dma, d.gidx)
                            key = d.sem
                            if wmap.get(key, -1) < d.val:
                                wmap[key] = d.val
                    if need_drain:
                        eng.drain()
                        last_drain = op.eidx - 1
                    for sem, val in wmap.items():
                        if waits.get(sem, -1) < val:
                            eng.wait_ge(sem, val)
                            waits[sem] = val
                    inst = op.fn(eng)
                    if op.sem is not None:
                        inst.then_inc(op.sem, 16 if op.dma else 1)
                return body
            return body

        blk.tensor(section("tensor"))
        blk.vector(section("vector"))
        blk.scalar(section("scalar"))
        blk.gpsimd(section("gpsimd"))
        blk.sync(section("sync"))
        blk_ctx.__exit__(None, None, None)


class Trk:

    def __init__(self, n=1):
        self.w = [None] * n
        self.r = [[] for _ in range(n)]

    def rd(self, slot=0):
        return [self.w[slot]] if self.w[slot] is not None else []

    def wr(self, slot=0):
        d = self.r[slot] + ([self.w[slot]] if self.w[slot] is not None else [])
        return d

    def did_rd(self, op, slot=0):
        self.r[slot].append(op)

    def did_wr(self, op, slot=0):
        self.w[slot] = op
        self.r[slot] = []


def _derived(cfg):
    d = dict(cfg)
    D, DKV, S, W, TT = cfg["D"], cfg["DKV"], cfg["S"], cfg["W"], cfg["TT"]
    d["T"] = S // 2
    d["KV"] = d["T"] + W
    d["NIC"] = D // 128
    d["NOC"] = D // 128
    d["NKC"] = DKV // 128
    d["NQH"] = D // 128
    d["NKVH"] = DKV // 128
    d["G"] = d["NQH"] // d["NKVH"]
    d["NQT"] = d["T"] // 128
    d["WKT"] = W // 128
    d["KSPAN"] = W + 128
    d["NKT"] = d["KV"] // 128
    d["NXT"] = d["KV"] // TT
    d["TSUB"] = TT // 128
    d["QXT0"] = W // TT
    assert W % TT == 0 and d["T"] % TT == 0 and TT % 128 == 0 and d["NIC"] % 4 == 0
    return d


def _mask_layout(c):
    WKT, NQT = c["WKT"], c["NQT"]
    off, plan = 0, []
    for qt in range(NQT):
        nfar = max(1, WKT - qt)
        plan.append((nfar, off, off + nfar * 128))
        off += (nfar + 1) * 128
    return off, plan


def build_mask_values(c, T0, mask_cols, plan):
    T, W, WKT, NQT = c["T"], c["W"], c["WKT"], c["NQT"]
    m = np.zeros((128, mask_cols), np.float32)
    di = np.arange(128)[:, None]
    dj = np.arange(128)[None, :]

    def blk(qt, k):
        i = T0 + 128 * qt + di
        j = T0 - W + 128 * (qt + k) + dj
        allowed = (j <= i) & (j > i - W) & (j >= 0)
        return allowed

    for qt in range(NQT):
        nfar, far_off, diag_off = plan[qt]
        for k in range(nfar):
            m[:, far_off + 128 * k: far_off + 128 * (k + 1)] = np.where(blk(qt, k), 0.0, NEG)
        for k in range(nfar, WKT):
            assert blk(qt, k).all(), (T0, qt, k)
        m[:, diag_off: diag_off + 128] = np.where(blk(qt, WKT), 0.0, NEG)
    return m


def build_rope_tables(c, T0):
    T, W = c["T"], c["W"]
    sm_scale = 1.0 / math.sqrt(128.0)
    inv_freq = ROPE_THETA ** (-np.arange(64, dtype=np.float64) / 64.0)

    def tables(pos, scale):
        ang = pos[None, :] * inv_freq[:, None]
        cos = np.cos(ang)
        sin = np.sin(ang)
        ct = np.concatenate([cos, cos], 0) * scale
        st = np.concatenate([-sin, sin], 0) * scale
        return ct.astype(np.float32), st.astype(np.float32)

    cq, sq = tables(np.arange(T0, T0 + T, dtype=np.float64), sm_scale)
    ck, sk = tables(np.arange(T0 - W, T0 + T, dtype=np.float64), 1.0)
    return cq, sq, ck, sk


def build_nc(cfg):
    c = _derived(cfg)
    D, DKV, T, KV, W, R, TT = c["D"], c["DKV"], c["T"], c["KV"], c["W"], c["R"], c["TT"]
    NIC, NOC, NKC = c["NIC"], c["NOC"], c["NKC"]
    NQT, WKT, KSPAN, NKT = c["NQT"], c["WKT"], c["KSPAN"], c["NKT"]
    NXT, TSUB, QXT0, NQH, NKVH, G = c["NXT"], c["TSUB"], c["QXT0"], c["NQH"], c["NKVH"], c["G"]
    mask_cols, mask_plan = _mask_layout(c)
    NSP = (KSPAN + 511) // 512
    NTG = (WKT + 1 + 3) // 4

    nc = bass.Bass("TRN2", target_bir_lowering=False, debug=False)

    xin = nc.dram_tensor("xin", [KV, D], F32, kind="ExternalInput").ap()
    wq_d = nc.dram_tensor("wq", [D, D], F32, kind="ExternalInput").ap()
    wk_d = nc.dram_tensor("wk", [D, DKV], F32, kind="ExternalInput").ap()
    wv_d = nc.dram_tensor("wv", [D, DKV], F32, kind="ExternalInput").ap()
    wo_d = nc.dram_tensor("wo", [D, D], F32, kind="ExternalInput").ap()
    wb_d = nc.dram_tensor("wb", [D, D], F32, kind="ExternalInput").ap()
    bb_d = nc.dram_tensor("bb", [NOC, 128], F32, kind="ExternalInput").ap()
    A_d = nc.dram_tensor("Ain", [R, D], F32, kind="ExternalInput").ap()
    WL_d = nc.dram_tensor("WLin", [R, D], F32, kind="ExternalInput").ap()
    B_d = nc.dram_tensor("Bin", [D, R], F32, kind="ExternalInput").ap()
    WR_d = nc.dram_tensor("WRin", [D, R], F32, kind="ExternalInput").ap()
    msk_d = nc.dram_tensor("msk", [128, mask_cols], F32, kind="ExternalInput").ap()
    cq_d = nc.dram_tensor("cq", [128, T], F32, kind="ExternalInput").ap()
    sq_d = nc.dram_tensor("sq", [128, T], F32, kind="ExternalInput").ap()
    ck_d = nc.dram_tensor("ck", [128, KV], F32, kind="ExternalInput").ap()
    sk_d = nc.dram_tensor("sk", [128, KV], F32, kind="ExternalInput").ap()
    ident_d = nc.dram_tensor("ident", [128, 128], F32, kind="ExternalInput").ap()
    rotm_d = nc.dram_tensor("rotm", [128, 128], F32, kind="ExternalInput").ap()

    out_d = nc.dram_tensor("out", [T, D], F32, kind="ExternalOutput").ap()

    BF16 = mybir.dt.bfloat16
    qhi_s = nc.dram_tensor("qhi_s", [D, T], BF16, kind="Internal").ap()
    qlo_s = nc.dram_tensor("qlo_s", [D, T], BF16, kind="Internal").ap()
    khi_s = nc.dram_tensor("khi_s", [DKV, KV], BF16, kind="Internal").ap()
    klo_s = nc.dram_tensor("klo_s", [DKV, KV], BF16, kind="Internal").ap()
    v_s = nc.dram_tensor("v_s", [KV, DKV], F32, kind="Internal").ap()
    o_s = nc.dram_tensor("o_s", [D, T], F32, kind="Internal").ap()
    xf_s = nc.dram_tensor("xf_s", [D, T], F32, kind="Internal").ap()
    y_s = nc.dram_tensor("y_s", [D, T], F32, kind="Internal").ap()
    qh_s = nc.dram_tensor("qh_s", [NQH, 128, NIC * 128], BF16, kind="Internal").ap()
    ql_s = nc.dram_tensor("ql_s", [NQH, 128, NIC * 128], BF16, kind="Internal").ap()
    kh_s = nc.dram_tensor("kh_s", [NKC, 128, NIC * 128], BF16, kind="Internal").ap()
    kl_s = nc.dram_tensor("kl_s", [NKC, 128, NIC * 128], BF16, kind="Internal").ap()
    vh_s = nc.dram_tensor("vh_s", [NKC, 128, NIC * 128], BF16, kind="Internal").ap()
    vl_s = nc.dram_tensor("vl_s", [NKC, 128, NIC * 128], BF16, kind="Internal").ap()
    oh_s = nc.dram_tensor("oh_s", [NOC, 128, NIC * 128], BF16, kind="Internal").ap()
    ol_s = nc.dram_tensor("ol_s", [NOC, 128, NIC * 128], BF16, kind="Internal").ap()
    wbth_s = nc.dram_tensor("wbth_s", [NOC, 128, NIC * 128], BF16, kind="Internal").ap()
    wbtl_s = nc.dram_tensor("wbtl_s", [NOC, 128, NIC * 128], BF16, kind="Internal").ap()

    pb = PB(nc)
    glob = ExitStack()
    ident = glob.enter_context(nc.sbuf_tensor("g_ident", [128, 128], F32))
    rotm = glob.enter_context(nc.sbuf_tensor("g_rotm", [128, 128], F32))
    b_sb = glob.enter_context(nc.sbuf_tensor("g_bb", [128, NOC], F32))
    awlt = glob.enter_context(nc.sbuf_tensor("g_awlt", [128, NIC * R], F32))
    bwrt = glob.enter_context(nc.sbuf_tensor("g_bwrt", [R, D], F32))
    consts = glob.enter_context(nc.sbuf_tensor("g_consts", [128, 2], F32))

    t_ident = Trk()
    t_rotm = Trk()
    t_bsb = Trk()
    t_awlt = Trk()
    t_bwrt = Trk()
    t_consts = Trk()

    with ExitStack() as p0:
        awl_r = p0.enter_context(nc.sbuf_tensor("p0_awl", [R, D], F32))
        wl_r = p0.enter_context(nc.sbuf_tensor("p0_wl", [R, D], F32))
        bwr_r = p0.enter_context(nc.sbuf_tensor("p0_bwr", [128, NOC * R], F32))
        wr_r = p0.enter_context(nc.sbuf_tensor("p0_wr", [128, NOC * R], F32))
        bb_r = p0.enter_context(nc.sbuf_tensor("p0_bbr", [NOC, 128], F32))
        wbrow = [p0.enter_context(nc.sbuf_tensor(f"p0_wbr{i}", [128, D], F32)) for i in range(4)]
        wbsh = [p0.enter_context(nc.sbuf_tensor(f"p0_wbh{i}", [128, 512], BF16)) for i in range(4)]
        wbsl = [p0.enter_context(nc.sbuf_tensor(f"p0_wbl{i}", [128, 512], BF16)) for i in range(4)]
        p_a = p0.enter_context(nc.psum_tensor("p0_pa", [128, 512], F32))
        p_b = p0.enter_context(nc.psum_tensor("p0_pb", [128, 512], F32))
        p_c = p0.enter_context(nc.psum_tensor("p0_pc", [128, 512], F32))
        p_d = p0.enter_context(nc.psum_tensor("p0_pd", [128, 512], F32))

        ld_ident = pb.add("sync", lambda e: e.dma_start(out=ident[:], in_=ident_d[:, :]), dma=True)
        t_ident.did_wr(ld_ident)
        ld_rot = pb.add("sync", lambda e: e.dma_start(out=rotm[:], in_=rotm_d[:, :]), dma=True)
        t_rotm.did_wr(ld_rot)
        cset0 = pb.add("vector", lambda e: e.memset(consts[:, 0:1], EXP_SHIFT))
        cset1 = pb.add("vector", lambda e: e.memset(consts[:, 1:2], LN_EPS), deps=[cset0])
        t_consts.did_wr(cset1)
        ld_bb = pb.add("sync", lambda e: e.dma_start(out=bb_r[:NOC, :], in_=bb_d[:, :]), dma=True)
        ld_a = pb.add("sync", lambda e: e.dma_start(out=awl_r[:R, :], in_=A_d[:, :]), dma=True)
        ld_wl = pb.add("sync", lambda e: e.dma_start(out=wl_r[:R, :], in_=WL_d[:, :]), dma=True)
        ld_b = pb.add("sync", lambda e: e.dma_start(
            out=bwr_r.ap().rearrange("p (c r) -> p c r", r=R),
            in_=B_d.rearrange("(c p) r -> p c r", p=128)), dma=True)
        ld_wr = pb.add("sync", lambda e: e.dma_start(
            out=wr_r.ap().rearrange("p (c r) -> p c r", r=R),
            in_=WR_d.rearrange("(c p) r -> p c r", p=128)), dma=True)

        add_awl = pb.add("vector", lambda e: e.tensor_tensor(
            awl_r[:R, :], awl_r[:R, :], wl_r[:R, :], op=ALU.add), deps=[ld_a, ld_wl])
        add_bwr = pb.add("vector", lambda e: e.tensor_tensor(
            bwr_r[:], bwr_r[:], wr_r[:], op=ALU.add), deps=[ld_b, ld_wr])

        t_pa = Trk()
        t_pb = Trk()
        tb = pb.add("tensor", lambda e: e.transpose(p_a[:, 0:NOC], bb_r[:NOC, :], ident[0:NOC, 0:NOC]),
                    deps=[ld_bb, ld_ident] + t_pa.wr())
        t_pa.did_wr(tb)
        ev = pb.add("vector", lambda e: e.tensor_copy(b_sb[:, 0:NOC], p_a[:, 0:NOC]), deps=[tb])
        t_pa.did_rd(ev)
        t_bsb.did_wr(ev)
        t_pc = Trk()
        t_pd = Trk()
        for ic in range(NIC):
            pp, tpp = (p_c, t_pc) if ic % 2 == 0 else (p_d, t_pd)
            tr = pb.add("tensor", lambda e, ic=ic, pp=pp: e.transpose(
                pp[:, 0:R], awl_r[:R, ic * 128:(ic + 1) * 128], ident[0:R, 0:R]),
                deps=[add_awl, ld_ident] + tpp.wr())
            tpp.did_wr(tr)
            ev = pb.add("vector", lambda e, ic=ic, pp=pp: e.tensor_copy(
                awlt[:, ic * R:(ic + 1) * R], pp[:, 0:R]), deps=[tr])
            tpp.did_rd(ev)
        t_awlt.did_wr(ev)
        for oc in range(NOC):
            pp, tpp = (p_c, t_pc) if oc % 2 == 0 else (p_d, t_pd)
            tr = pb.add("tensor", lambda e, oc=oc, pp=pp: e.transpose(
                pp[:R, 0:128], bwr_r[:, oc * R:(oc + 1) * R], ident[:, :]),
                deps=[add_bwr, ld_ident] + tpp.wr())
            tpp.did_wr(tr)
            ev = pb.add("vector", lambda e, oc=oc, pp=pp: e.tensor_copy(
                bwrt[:R, oc * 128:(oc + 1) * 128], pp[:R, 0:128]), deps=[tr])
            tpp.did_rd(ev)
        t_bwrt.did_wr(ev)

        t_row = Trk(4)
        t_stg = Trk(4)
        NB4 = NIC // 4
        for oc in range(NOC):
            ld = pb.add("sync", lambda e, oc=oc: e.dma_start(
                out=wbrow[oc % 4][:], in_=wb_d[oc * 128:(oc + 1) * 128, :]),
                deps=t_row.wr(oc % 4), dma=True)
            t_row.did_wr(ld, oc % 4)
            for g in range(NB4):
                pt = p_a if g % 2 == 0 else p_b
                tp = t_pa if g % 2 == 0 else t_pb
                trs = []
                for t4 in range(4):
                    ic = g * 4 + t4
                    tr = pb.add("tensor", lambda e, oc=oc, ic=ic, t4=t4, pt=pt: e.transpose(
                        pt[:, t4 * 128:(t4 + 1) * 128],
                        wbrow[oc % 4][:, ic * 128:(ic + 1) * 128], ident[:, :]),
                        deps=[ld, ld_ident] + (tp.wr() if t4 == 0 else []))
                    trs.append(tr)
                    t_row.did_rd(tr, oc % 4)
                tp.did_wr(trs[-1])
                for tr in trs[:-1]:
                    tp.r[0].append(tr)
                sl = (oc * NB4 + g) % 4
                evh = pb.add("scalar", lambda e, sl=sl, pt=pt: e.activation(
                    wbsh[sl][:], pt[:], AF.Copy, bias=0.0, scale=1.0),
                    deps=trs + t_stg.wr(sl))
                tp.did_rd(evh)
                evl = pb.add("vector", lambda e, sl=sl, pt=pt: e.tensor_tensor(
                    wbsl[sl][:], pt[:], wbsh[sl][:], op=ALU.subtract),
                    deps=trs + [evh] + t_stg.wr(sl))
                tp.did_rd(evl)
                t_stg.did_wr(evl, sl)
                t_stg.r[sl].append(evh)
                sth = pb.add("sync", lambda e, oc=oc, g=g, sl=sl: e.dma_start(
                    out=wbth_s[oc, :, g * 512:(g + 1) * 512], in_=wbsh[sl][:]),
                    deps=[evh, evl], dma=True)
                stl = pb.add("sync", lambda e, oc=oc, g=g, sl=sl: e.dma_start(
                    out=wbtl_s[oc, :, g * 512:(g + 1) * 512], in_=wbsl[sl][:]),
                    deps=[evl], dma=True)
                t_stg.did_rd(sth, sl)
                t_stg.did_rd(stl, sl)
        pb.barrier()

    with ExitStack() as p1:
        xtm = [p1.enter_context(nc.sbuf_tensor(f"p1_xtm{i}", [128, D], F32)) for i in range(2)]
        xfh = p1.enter_context(nc.sbuf_tensor("p1_xfh", [128, NIC * TT], BF16))
        xfl = p1.enter_context(nc.sbuf_tensor("p1_xfl", [128, NIC * TT], BF16))
        xst = [p1.enter_context(nc.sbuf_tensor(f"p1_xst{i}", [128, 512], F32)) for i in range(2)]
        wfh = [p1.enter_context(nc.sbuf_tensor(f"p1_wh{i}", [128, NIC * 128], BF16)) for i in range(2)]
        wfl = [p1.enter_context(nc.sbuf_tensor(f"p1_wl{i}", [128, NIC * 128], BF16)) for i in range(2)]
        wf32 = p1.enter_context(nc.sbuf_tensor("p1_w32", [128, NIC * 128], F32))
        qsb = [p1.enter_context(nc.sbuf_tensor(f"p1_q{i}", [128, TT], F32)) for i in range(2)]
        qout = [p1.enter_context(nc.sbuf_tensor(f"p1_qo{i}", [128, TT], F32)) for i in range(2)]
        qhb = [p1.enter_context(nc.sbuf_tensor(f"p1_qh{i}", [128, TT], BF16)) for i in range(2)]
        qlb = [p1.enter_context(nc.sbuf_tensor(f"p1_ql{i}", [128, TT], BF16)) for i in range(2)]
        t1b = p1.enter_context(nc.sbuf_tensor("p1_t1", [128, TT], F32))
        t2b = p1.enter_context(nc.sbuf_tensor("p1_t2", [128, TT], F32))
        vstg = [p1.enter_context(nc.sbuf_tensor(f"p1_vs{i}", [128, 256], F32)) for i in range(2)]
        rtq = p1.enter_context(nc.sbuf_tensor("p1_rtq", [128, 2 * TT], F32))
        rtk = p1.enter_context(nc.sbuf_tensor("p1_rtk", [128, 2 * TT], F32))
        p_x = [p1.enter_context(nc.psum_tensor(f"p1_px{i}", [128, 512], F32)) for i in range(2)]
        p_q = [p1.enter_context(nc.psum_tensor(f"p1_pq{i}", [128, TT], F32)) for i in range(2)]
        p_r = [p1.enter_context(nc.psum_tensor(f"p1_pr{i}", [128, TT], F32)) for i in range(2)]
        p_v = [p1.enter_context(nc.psum_tensor(f"p1_pv{i}", [128, 256], F32)) for i in range(2)]

        t_xtm = Trk(2)
        t_xfm = Trk()
        t_xst = Trk(2)
        t_wb = Trk(2)
        t_w32 = Trk()
        t_qsb = Trk(2)
        t_qout = Trk(2)
        t_qhb = Trk(2)
        t_qlb = Trk(2)
        t_t1 = Trk()
        t_t2 = Trk()
        t_vstg = Trk(2)
        t_rtq = Trk()
        t_rtk = Trk()
        t_px = [Trk() for _ in range(2)]
        t_pq = [Trk() for _ in range(2)]
        t_pr = [Trk() for _ in range(2)]
        t_pv = [Trk() for _ in range(2)]
        t_ws = {"q": Trk(), "k": Trk(), "v": Trk()}

        wsc = {"q": (wq_d, qh_s, ql_s), "k": (wk_d, kh_s, kl_s), "v": (wv_d, vh_s, vl_s)}

        wi = 0
        for xt in range(NXT):
            t0 = xt * TT
            qr = xt >= QXT0
            tq0 = t0 - W
            xfm_wr = []
            for sub in range(TSUB):
                par = (xt * TSUB + sub) % 2
                ld = pb.add("sync", lambda e, t0=t0, sub=sub, par=par: e.dma_start(
                    out=xtm[par][:], in_=xin[t0 + sub * 128: t0 + (sub + 1) * 128, :]),
                    deps=t_xtm.wr(par), dma=True)
                t_xtm.did_wr(ld, par)
                for g in range(NIC // 4):
                    gp = g % 2
                    trs = []
                    for t4 in range(4):
                        ic = g * 4 + t4
                        tr = pb.add("tensor", lambda e, par=par, ic=ic, t4=t4, gp=gp: e.transpose(
                            p_x[gp][:, t4 * 128:(t4 + 1) * 128],
                            xtm[par][:, ic * 128:(ic + 1) * 128], ident[:, :]),
                            deps=[ld] + (t_px[gp].wr() if t4 == 0 else []))
                        t_ident.did_rd(tr)
                        t_xtm.did_rd(tr, par)
                        trs.append(tr)
                    t_px[gp].did_wr(trs[-1])
                    t_px[gp].r[0].extend(trs[:-1])
                    xv = lambda t_: t_.ap().rearrange("p (ic t) -> p ic t", t=TT)
                    evh = pb.add("scalar", lambda e, g=g, sub=sub, gp=gp, xv=xv: e.activation(
                        xv(xfh)[:, g * 4:(g + 1) * 4, sub * 128:(sub + 1) * 128],
                        p_x[gp][:], AF.Copy, bias=0.0, scale=1.0),
                        deps=trs + (t_xfm.wr() if (sub == 0 and g == 0) else []))
                    t_px[gp].did_rd(evh)
                    evl = pb.add("vector", lambda e, g=g, sub=sub, gp=gp, xv=xv: e.tensor_tensor(
                        xv(xfl)[:, g * 4:(g + 1) * 4, sub * 128:(sub + 1) * 128],
                        p_x[gp][:],
                        xv(xfh)[:, g * 4:(g + 1) * 4, sub * 128:(sub + 1) * 128],
                        op=ALU.subtract),
                        deps=trs + [evh])
                    t_px[gp].did_rd(evl)
                    xfm_wr.extend([evh, evl])
                    if qr:
                        ev32 = pb.add("vector", lambda e, gp=gp: e.tensor_copy(
                            xst[gp][:], p_x[gp][:]), deps=trs + t_xst.wr(gp))
                        t_px[gp].did_rd(ev32)
                        t_xst.did_wr(ev32, gp)
                        st32 = pb.add("gpsimd", lambda e, g=g, sub=sub, tq0=tq0, gp=gp: e.dma_start(
                            out=xf_s.rearrange("(ic p) t -> p ic t", p=128)
                            [:, g * 4:(g + 1) * 4, tq0 + sub * 128: tq0 + (sub + 1) * 128],
                            in_=xst[gp].ap().rearrange("p (ic t) -> p ic t", t=128)),
                            deps=[ev32], dma=True)
                        t_xst.did_rd(st32, gp)
            t_xfm.did_wr(xfm_wr[-1])
            t_xfm.r[0].extend(xfm_wr[:-1])
            lrk = pb.add("sync", lambda e, t0=t0: e.dma_start(
                out=rtk[:, 0:TT], in_=ck_d[:, t0:t0 + TT]), deps=t_rtk.wr(), dma=True)
            lrk2 = pb.add("sync", lambda e, t0=t0: e.dma_start(
                out=rtk[:, TT:2 * TT], in_=sk_d[:, t0:t0 + TT]), deps=[], dma=True)
            t_rtk.did_wr(lrk)
            t_rtk.r[0].append(lrk2)
            if qr:
                lrq = pb.add("sync", lambda e, tq0=tq0: e.dma_start(
                    out=rtq[:, 0:TT], in_=cq_d[:, tq0:tq0 + TT]), deps=t_rtq.wr(), dma=True)
                lrq2 = pb.add("sync", lambda e, tq0=tq0: e.dma_start(
                    out=rtq[:, TT:2 * TT], in_=sq_d[:, tq0:tq0 + TT]), deps=[], dma=True)
                t_rtq.did_wr(lrq)
                t_rtq.r[0].append(lrq2)

            steps = []
            if qr:
                steps += [("q", oc) for oc in range(NQH)]
            steps += [("k", oc) for oc in range(NKC)]
            steps += [("v", oc) for oc in range(NKC)]
            for kind, oc in steps:
                wp = wi % 2
                wraw, whs, wls = wsc[kind]
                first_xt = QXT0 if kind == "q" else 0
                cw = 128
                if xt == first_xt:
                    ldw32 = pb.add("sync", lambda e, wraw=wraw, oc=oc, cw=cw: e.dma_start(
                        out=wf32.ap().rearrange("p (ic n) -> p ic n", n=cw)[:, :, 0:cw] if cw != 128
                        else wf32.ap().rearrange("p (ic n) -> p ic n", n=128),
                        in_=wraw.rearrange("(ic p) n -> p ic n", p=128)[:, :, oc * cw:(oc + 1) * cw]),
                        deps=t_w32.wr(), dma=True)
                    t_w32.did_wr(ldw32)
                    qs_ops = []
                    QW = NIC * cw // 4
                    for qi in range(4):
                        sph = pb.add("scalar", lambda e, wp=wp, qi=qi, QW=QW: e.activation(
                            wfh[wp][:, qi * QW:(qi + 1) * QW], wf32[:, qi * QW:(qi + 1) * QW],
                            AF.Copy, bias=0.0, scale=1.0),
                            deps=[ldw32] + (t_wb.wr(wp) if qi == 0 else []))
                        t_w32.did_rd(sph)
                        spl = pb.add("vector", lambda e, wp=wp, qi=qi, QW=QW: e.tensor_tensor(
                            wfl[wp][:, qi * QW:(qi + 1) * QW], wf32[:, qi * QW:(qi + 1) * QW],
                            wfh[wp][:, qi * QW:(qi + 1) * QW], op=ALU.subtract),
                            deps=[ldw32, sph] + (t_wb.wr(wp) if qi == 0 else []))
                        t_w32.did_rd(spl)
                        qs_ops.extend([sph, spl])
                    t_wb.did_wr(qs_ops[-1], wp)
                    t_wb.r[wp].extend(qs_ops[:-1])
                    ldw = qs_ops[-1]
                    ldwh = qs_ops[-2]
                    sth = pb.add("gpsimd", lambda e, whs=whs, oc=oc, wp=wp: e.dma_start(
                        out=whs[oc], in_=wfh[wp][:]), deps=[sph, spl], dma=True)
                    stl = pb.add("gpsimd", lambda e, wls=wls, oc=oc, wp=wp: e.dma_start(
                        out=wls[oc], in_=wfl[wp][:]), deps=[spl], dma=True)
                    t_wb.did_rd(sth, wp)
                    t_wb.did_rd(stl, wp)
                    t_ws[kind].did_wr(stl)
                    t_ws[kind].r[0].append(sth)
                else:
                    ldwh = pb.add("sync", lambda e, whs=whs, oc=oc, wp=wp: e.dma_start(
                        out=wfh[wp][:], in_=whs[oc]),
                        deps=t_wb.wr(wp) + t_ws[kind].rd(), dma=True)
                    ldw = pb.add("sync", lambda e, wls=wls, oc=oc, wp=wp: e.dma_start(
                        out=wfl[wp][:], in_=wls[oc]),
                        deps=t_wb.wr(wp) + t_ws[kind].rd(), dma=True)
                    t_ws[kind].did_rd(ldwh)
                    t_ws[kind].did_rd(ldw)
                    t_wb.did_wr(ldw, wp)
                    t_wb.r[wp].append(ldwh)
                wi += 1
                if kind in ("q", "k"):
                    pq = wi % 2
                    NHALF = 1
                    HW_ = TT // NHALF
                    mms = []
                    for ic in range(NIC):
                        for ps_, wb_, xb_ in ((0, wfh, xfh), (1, wfh, xfl), (2, wfl, xfh)):
                            for hf in range(NHALF):
                                mm = pb.add("tensor", lambda e, ic=ic, pq=pq, wp=wp, wb_=wb_, xb_=xb_, ps_=ps_, hf=hf: e.matmul(
                                    p_q[pq][:, hf * HW_:(hf + 1) * HW_],
                                    wb_[wp][:, ic * 128:(ic + 1) * 128],
                                    xb_[:, ic * TT + hf * HW_: ic * TT + (hf + 1) * HW_],
                                    start=(ic == 0 and ps_ == 0 and hf == 0),
                                    stop=(ic == NIC - 1 and ps_ == 2 and hf == NHALF - 1)),
                                    deps=[ldw, ldwh] + xfm_wr + (t_pq[pq].wr() if (ic == 0 and ps_ == 0 and hf == 0) else []))
                                t_wb.did_rd(mm, wp)
                                t_xfm.did_rd(mm)
                                mms.append(mm)
                    t_pq[pq].did_wr(mms[-1])
                    t_pq[pq].r[0].extend(mms[:-1])
                    evq = pb.add("vector", lambda e, pq=pq: e.tensor_copy(qsb[pq][:], p_q[pq][:]),
                                 deps=mms + t_qsb.wr(pq))
                    t_pq[pq].did_rd(evq)
                    t_qsb.did_wr(evq, pq)
                    rot = pb.add("tensor", lambda e, pq=pq: e.matmul(
                        p_r[pq][:], rotm[:], qsb[pq][:], start=True, stop=True),
                        deps=[evq, ld_rot] + t_pr[pq].wr())
                    t_qsb.did_rd(rot, pq)
                    t_pr[pq].did_wr(rot)
                    rt_ = rtq if kind == "q" else rtk
                    trt = t_rtq if kind == "q" else t_rtk
                    m1 = pb.add("vector", lambda e, pq=pq, rt_=rt_: e.tensor_tensor(
                        t1b[:], qsb[pq][:], rt_[:, 0:TT], op=ALU.mult),
                        deps=[evq, (lrq if kind == "q" else lrk)] + t_t1.wr())
                    t_qsb.did_rd(m1, pq)
                    trt.did_rd(m1)
                    t_t1.did_wr(m1)
                    m2 = pb.add("vector", lambda e, pq=pq, rt_=rt_: e.tensor_tensor(
                        t2b[:], p_r[pq][:], rt_[:, TT:2 * TT], op=ALU.mult),
                        deps=[rot, (lrq2 if kind == "q" else lrk2)] + t_t2.wr())
                    t_pr[pq].did_rd(m2)
                    trt.did_rd(m2)
                    t_t2.did_wr(m2)
                    cmb = pb.add("vector", lambda e, pq=pq: e.tensor_tensor(
                        qout[pq][:], t1b[:], t2b[:], op=ALU.add),
                        deps=[m1, m2] + t_qout.wr(pq))
                    t_t1.did_rd(cmb)
                    t_t2.did_rd(cmb)
                    t_qout.did_wr(cmb, pq)
                    dsth = qhi_s if kind == "q" else khi_s
                    dstl = qlo_s if kind == "q" else klo_s
                    dc0 = tq0 if kind == "q" else t0
                    qsh = pb.add("scalar", lambda e, pq=pq: e.activation(
                        qhb[pq][:], qout[pq][:], AF.Copy, bias=0.0, scale=1.0),
                        deps=[cmb] + t_qhb.wr(pq))
                    t_qout.did_rd(qsh, pq)
                    t_qhb.did_wr(qsh, pq)
                    qsl = pb.add("vector", lambda e, pq=pq: e.tensor_tensor(
                        qlb[pq][:], qout[pq][:], qhb[pq][:], op=ALU.subtract),
                        deps=[cmb, qsh] + t_qlb.wr(pq))
                    t_qout.did_rd(qsl, pq)
                    t_qhb.did_rd(qsl, pq)
                    t_qlb.did_wr(qsl, pq)
                    sth = pb.add("gpsimd", lambda e, dsth=dsth, oc=oc, dc0=dc0, pq=pq: e.dma_start(
                        out=dsth[oc * 128:(oc + 1) * 128, dc0:dc0 + TT], in_=qhb[pq][:]),
                        deps=[qsh], dma=True)
                    t_qhb.did_rd(sth, pq)
                    stl = pb.add("gpsimd", lambda e, dstl=dstl, oc=oc, dc0=dc0, pq=pq: e.dma_start(
                        out=dstl[oc * 128:(oc + 1) * 128, dc0:dc0 + TT], in_=qlb[pq][:]),
                        deps=[qsl], dma=True)
                    t_qlb.did_rd(stl, pq)
                else:
                    pq = wi % 2
                    NHALF = 1
                    HW_ = TT // NHALF
                    mms = []
                    for ic in range(NIC):
                        for ps_, wb_, xb_ in ((0, wfh, xfh), (1, wfh, xfl), (2, wfl, xfh)):
                            for hf in range(NHALF):
                                mm = pb.add("tensor", lambda e, ic=ic, pq=pq, wp=wp, wb_=wb_, xb_=xb_, ps_=ps_, hf=hf: e.matmul(
                                    p_q[pq][:, hf * HW_:(hf + 1) * HW_],
                                    wb_[wp][:, ic * 128:(ic + 1) * 128],
                                    xb_[:, ic * TT + hf * HW_: ic * TT + (hf + 1) * HW_],
                                    start=(ic == 0 and ps_ == 0 and hf == 0),
                                    stop=(ic == NIC - 1 and ps_ == 2 and hf == NHALF - 1)),
                                    deps=[ldw, ldwh] + xfm_wr + (t_pq[pq].wr() if (ic == 0 and ps_ == 0 and hf == 0) else []))
                                t_wb.did_rd(mm, wp)
                                t_xfm.did_rd(mm)
                                mms.append(mm)
                    t_pq[pq].did_wr(mms[-1])
                    t_pq[pq].r[0].extend(mms[:-1])
                    evv = pb.add("vector", lambda e, pq=pq: e.tensor_copy(qsb[pq][:], p_q[pq][:]),
                                 deps=mms + t_qsb.wr(pq))
                    t_pq[pq].did_rd(evv)
                    t_qsb.did_wr(evv, pq)
                    for sub in range(TSUB):
                        pv = sub % 2
                        tr = pb.add("tensor", lambda e, pq=pq, sub=sub, pv=pv: e.transpose(
                            p_v[pv][:, 0:128], qsb[pq][:, sub * 128:(sub + 1) * 128], ident[:, :]),
                            deps=[evv] + t_pv[pv].wr())
                        t_qsb.did_rd(tr, pq)
                        t_pv[pv].did_wr(tr)
                        ev = pb.add("vector", lambda e, pv=pv: e.tensor_copy(
                            vstg[pv][:, 0:128], p_v[pv][:, 0:128]),
                            deps=[tr] + t_vstg.wr(pv))
                        t_pv[pv].did_rd(ev)
                        t_vstg.did_wr(ev, pv)
                        st = pb.add("gpsimd", lambda e, t0=t0, sub=sub, oc=oc, pv=pv: e.dma_start(
                            out=v_s[t0 + sub * 128: t0 + (sub + 1) * 128, oc * 128:(oc + 1) * 128],
                            in_=vstg[pv][:, 0:128]), deps=[ev], dma=True)
                        t_vstg.did_rd(st, pv)
        pb.barrier()

    with ExitStack() as p2:
        mskb = p2.enter_context(nc.sbuf_tensor("p2_msk", [128, mask_cols], F32))
        khb = p2.enter_context(nc.sbuf_tensor("p2_kh", [128, KV], BF16))
        klb = p2.enter_context(nc.sbuf_tensor("p2_kl", [128, KV], BF16))
        vh = p2.enter_context(nc.sbuf_tensor("p2_vh", [128, NKT * 128], F32))
        qhhb = p2.enter_context(nc.sbuf_tensor("p2_qhh", [128, G * T], BF16))
        qhlb = p2.enter_context(nc.sbuf_tensor("p2_qhl", [128, G * T], BF16))
        usb = [p2.enter_context(nc.sbuf_tensor(f"p2_u{i}", [128, KSPAN], F32)) for i in range(2)]
        zsb = [p2.enter_context(nc.sbuf_tensor(f"p2_z{i}", [128, 1], F32)) for i in range(2)]
        zrb = [p2.enter_context(nc.sbuf_tensor(f"p2_zr{i}", [128, 1], F32)) for i in range(2)]
        ut4 = [p2.enter_context(nc.sbuf_tensor(f"p2_ut{i}", [128, (WKT + 1) * G * 128], F32)) for i in range(2)]
        ostg = [p2.enter_context(nc.sbuf_tensor(f"p2_o{i}", [128, G * 128], F32)) for i in range(2)]
        p_s = p2.enter_context(nc.psum_tensor("p2_ps", [128, 1536], F32))
        p_t = [p2.enter_context(nc.psum_tensor(f"p2_pt{i}", [128, 256], F32)) for i in range(2)]
        p_o = [p2.enter_context(nc.psum_tensor(f"p2_po{i}", [128, G * 128], F32)) for i in range(2)]

        t_msk = Trk()
        t_kh = Trk()
        t_vh = Trk()
        t_qh = Trk()
        t_u = Trk(2)
        t_z = Trk(2)
        t_zr = Trk(2)
        t_ut = Trk(2)
        t_os = Trk(2)
        t_ps = Trk()
        t_pt = Trk(2)
        t_po = [Trk() for _ in range(2)]

        ldm = pb.add("sync", lambda e: e.dma_start(out=mskb[:], in_=msk_d[:, :]), dma=True)
        t_msk.did_wr(ldm)

        GW = G * 128

        def p2_scores_exp(qt, g, up, ldq, ldk):
            nfar, far_off, diag_off = mask_plan[qt]
            mms = []
            for sp in range(NSP):
                c0 = sp * 512
                cw = min(512, KSPAN - c0)
                for pi, (qb, kb) in enumerate(((qhhb, khb), (qhhb, klb), (qhlb, khb))):
                    mm = pb.add("tensor", lambda e, g=g, qt=qt, c0=c0, cw=cw, up=up, qb=qb, kb=kb, pi=pi: e.matmul(
                        p_s[:, c0:c0 + cw],
                        qb[:, g * T + qt * 128: g * T + (qt + 1) * 128],
                        kb[:, qt * 128 + c0: qt * 128 + c0 + cw],
                        start=(pi == 0), stop=(pi == 2)),
                        deps=ldq + ldk + (t_ps.wr() if (sp == 0 and pi == 0) else []))
                    t_qh.did_rd(mm)
                    t_kh.did_rd(mm)
                    mms.append(mm)
            t_ps.did_wr(mms[-1])
            t_ps.r[0].extend(mms[:-1])
            mk1 = pb.add("vector", lambda e, nfar=nfar, far_off=far_off: e.tensor_tensor(
                p_s[:, 0:nfar * 128], p_s[:, 0:nfar * 128],
                mskb[:, far_off:far_off + nfar * 128], op=ALU.add),
                deps=mms + [ldm])
            mk2 = pb.add("vector", lambda e, diag_off=diag_off: e.tensor_tensor(
                p_s[:, WKT * 128:(WKT + 1) * 128], p_s[:, WKT * 128:(WKT + 1) * 128],
                mskb[:, diag_off:diag_off + 128], op=ALU.add),
                deps=mms + [ldm])
            t_msk.did_rd(mk1)
            t_msk.did_rd(mk2)
            t_ps.did_wr(mk2)
            t_ps.r[0].append(mk1)
            ex = pb.add("scalar", lambda e, up=up: e.activation(
                usb[up][:], p_s[:, 0:KSPAN], AF.Exp,
                bias=consts[:, 0:1], scale=1.0, accum_out=zsb[up][:]),
                deps=[mk1, mk2] + t_u.wr(up) + t_z.wr(up) + t_consts.rd())
            t_ps.did_rd(ex)
            t_consts.did_rd(ex)
            t_u.did_wr(ex, up)
            t_z.did_wr(ex, up)
            rc = pb.add("vector", lambda e, up=up: e.reciprocal(zrb[up][:], zsb[up][:]),
                        deps=[ex] + t_zr.wr(up))
            t_z.did_rd(rc, up)
            t_zr.did_wr(rc, up)
            un = pb.add("vector", lambda e, up=up: e.tensor_scalar(
                usb[up][:], usb[up][:], zrb[up][:, 0:1], None, op0=ALU.mult),
                deps=[ex, rc])
            t_zr.did_rd(un, up)
            t_u.did_wr(un, up)
            return un

        def p2_transp(g, up, jp, un, first_of_block):
            ut_wr = []
            for tg in range((WKT + 1 + 1) // 2):
                hp = tg % 2
                kts = [kt for kt in (2 * tg, 2 * tg + 1) if kt < WKT + 1]
                trs = []
                for j, kt in enumerate(kts):
                    tr = pb.add("tensor", lambda e, kt=kt, j=j, hp=hp, up=up: e.transpose(
                        p_t[hp][:, j * 128:(j + 1) * 128],
                        usb[up][:, kt * 128:(kt + 1) * 128], ident[:, :]),
                        deps=[un] + (t_pt.wr(hp) if j == 0 else []))
                    t_u.did_rd(tr, up)
                    t_ident.did_rd(tr)
                    trs.append(tr)
                t_pt.did_wr(trs[-1], hp)
                t_pt.r[hp].extend(trs[:-1])
                k0, nk = kts[0], len(kts)
                wrdep = (t_ut.wr(jp) if (first_of_block and tg == 0) else [])
                if tg % 2 == 0:
                    ev = pb.add("vector", lambda e, k0=k0, nk=nk, hp=hp, jp=jp, g=g: e.tensor_copy(
                        ut4[jp].ap().rearrange("p (kt c) -> p kt c", c=GW)
                        [:, k0:k0 + nk, g * 128:(g + 1) * 128],
                        p_t[hp][:, 0:nk * 128].rearrange("p (kt c) -> p kt c", c=128)),
                        deps=trs + wrdep)
                else:
                    ev = pb.add("scalar", lambda e, k0=k0, nk=nk, hp=hp, jp=jp, g=g: e.activation(
                        ut4[jp].ap().rearrange("p (kt c) -> p kt c", c=GW)
                        [:, k0:k0 + nk, g * 128:(g + 1) * 128],
                        p_t[hp][:, 0:nk * 128].rearrange("p (kt c) -> p kt c", c=128),
                        AF.Copy, bias=0.0, scale=1.0),
                        deps=trs + wrdep)
                t_pt.did_rd(ev, hp)
                ut_wr.append(ev)
            return ut_wr

        def p2_av_chunk(kvh, qt, jp, kts, ut_deps, ldv):
            avs = []
            for kt in kts:
                mm = pb.add("tensor", lambda e, kt=kt, qt=qt, jp=jp: e.matmul(
                    p_o[jp][:],
                    vh[:, (qt + kt) * 128:(qt + kt + 1) * 128],
                    ut4[jp][:, kt * GW:(kt + 1) * GW],
                    start=(kt == 0), stop=(kt == WKT)),
                    deps=ut_deps + ldv + (t_po[jp].wr() if kt == 0 else []))
                t_ut.did_rd(mm, jp)
                t_vh.did_rd(mm)
                if kt == 0:
                    t_po[jp].did_wr(mm)
                else:
                    t_po[jp].r[0].append(mm)
                avs.append(mm)
            return avs

        def p2_o_out(kvh, qt, jp, avs):
            ev = pb.add("vector", lambda e, jp=jp: e.tensor_copy(ostg[jp][:], p_o[jp][:]),
                        deps=avs + t_os.wr(jp))
            t_po[jp].did_rd(ev)
            t_os.did_wr(ev, jp)
            st = pb.add("gpsimd", lambda e, qt=qt, kvh=kvh, jp=jp: e.dma_start(
                out=o_s.rearrange("(h p) t -> p h t", p=128)
                [:, kvh * G:(kvh + 1) * G, qt * 128:(qt + 1) * 128],
                in_=ostg[jp].ap().rearrange("p (h q) -> p h q", q=128)),
                deps=[ev], dma=True)
            t_os.did_rd(st, jp)

        NKT1 = WKT + 1
        _bnds = [round(i * NKT1 / G) for i in range(G + 1)]
        AVSPLIT = [list(range(_bnds[i], _bnds[i + 1])) for i in range(G)]
        blocks = [(kvh, qt) for kvh in range(NKVH) for qt in range(NQT)]
        prev = None
        pend_tr = None
        cur_kvh = -1
        ldq = ldk = ldv = None

        cur_ut = []
        for bi, (kvh, qt) in enumerate(blocks):
            if kvh != cur_kvh:
                if pend_tr is not None:
                    g_, up_, jp_, un_, fob_ = pend_tr
                    cur_ut.extend(p2_transp(g_, up_, jp_, un_, fob_))
                    pend_tr = None
                if prev is not None:
                    pkvh, pqt, pjp, put, pldv, pavs = prev
                    while pavs[0] < G:
                        pavs[1].extend(p2_av_chunk(pkvh, pqt, pjp, AVSPLIT[pavs[0]], put, pldv))
                        pavs[0] += 1
                    p2_o_out(pkvh, pqt, pjp, pavs[1])
                    prev = None
                cur_kvh = kvh
                l1 = pb.add("sync", lambda e, kvh=kvh: e.dma_start(
                    out=khb[:], in_=khi_s[kvh * 128:(kvh + 1) * 128, :]), deps=t_kh.wr(), dma=True)
                l2 = pb.add("sync", lambda e, kvh=kvh: e.dma_start(
                    out=klb[:], in_=klo_s[kvh * 128:(kvh + 1) * 128, :]), deps=t_kh.wr(), dma=True)
                t_kh.did_wr(l2)
                t_kh.r[0].append(l1)
                ldk = [l1, l2]
                l3 = pb.add("sync", lambda e, kvh=kvh: e.dma_start(
                    out=vh.ap().rearrange("p (kt c) -> p kt c", c=128),
                    in_=v_s.rearrange("(kt p) d -> p kt d", p=128)[:, :, kvh * 128:(kvh + 1) * 128]),
                    deps=t_vh.wr(), dma=True)
                t_vh.did_wr(l3)
                ldv = [l3]
                l4 = pb.add("sync", lambda e, kvh=kvh: e.dma_start(
                    out=qhhb.ap().rearrange("p (h t) -> p h t", t=T),
                    in_=qhi_s.rearrange("(h p) t -> p h t", p=128)[:, kvh * G:(kvh + 1) * G, :]),
                    deps=t_qh.wr(), dma=True)
                l5 = pb.add("sync", lambda e, kvh=kvh: e.dma_start(
                    out=qhlb.ap().rearrange("p (h t) -> p h t", t=T),
                    in_=qlo_s.rearrange("(h p) t -> p h t", p=128)[:, kvh * G:(kvh + 1) * G, :]),
                    deps=t_qh.wr(), dma=True)
                t_qh.did_wr(l5)
                t_qh.r[0].append(l4)
                ldq = [l4, l5]
            jp = bi % 2
            blk_ut = []
            for g in range(G):
                un = p2_scores_exp(qt, g, g % 2, ldq, ldk)
                if pend_tr is not None:
                    g_, up_, jp_, un_, fob_ = pend_tr
                    tgt = blk_ut if jp_ == jp else cur_ut
                    tgt.extend(p2_transp(g_, up_, jp_, un_, fob_))
                    pend_tr = None
                if prev is not None:
                    pkvh, pqt, pjp, put, pldv, pavs = prev
                    nch = pavs[0]
                    pavs[1].extend(p2_av_chunk(pkvh, pqt, pjp, AVSPLIT[nch], put, pldv))
                    pavs[0] += 1
                    if pavs[0] == G:
                        p2_o_out(pkvh, pqt, pjp, pavs[1])
                        prev = None
                pend_tr = (g, g % 2, jp, un, g == 0)
            if prev is not None:
                pkvh, pqt, pjp, put, pldv, pavs = prev
                while pavs[0] < G:
                    pavs[1].extend(p2_av_chunk(pkvh, pqt, pjp, AVSPLIT[pavs[0]], put, pldv))
                    pavs[0] += 1
                p2_o_out(pkvh, pqt, pjp, pavs[1])
                prev = None
            cur_ut = blk_ut
            prev = (kvh, qt, jp, cur_ut, ldv, [0, []])
        if pend_tr is not None:
            g_, up_, jp_, un_, fob_ = pend_tr
            cur_ut.extend(p2_transp(g_, up_, jp_, un_, fob_))
            pend_tr = None
        if prev is not None:
            pkvh, pqt, pjp, put, pldv, pavs = prev
            while pavs[0] < G:
                pavs[1].extend(p2_av_chunk(pkvh, pqt, pjp, AVSPLIT[pavs[0]], put, pldv))
                pavs[0] += 1
            p2_o_out(pkvh, pqt, pjp, pavs[1])
            prev = None
        pb.barrier()

    TT3 = min(TT, 256)
    TSUB3 = TT3 // 128
    with ExitStack() as p3:
        ob32 = p3.enter_context(nc.sbuf_tensor("p3_ob32", [128, NIC * TT3], F32))
        ofh = p3.enter_context(nc.sbuf_tensor("p3_ofh", [128, NIC * TT3], BF16))
        ofl = p3.enter_context(nc.sbuf_tensor("p3_ofl", [128, NIC * TT3], BF16))
        w3h = [p3.enter_context(nc.sbuf_tensor(f"p3_wh{i}", [128, NIC * 128], BF16)) for i in range(2)]
        w3l = [p3.enter_context(nc.sbuf_tensor(f"p3_wl{i}", [128, NIC * 128], BF16)) for i in range(2)]
        w332 = p3.enter_context(nc.sbuf_tensor("p3_w32", [128, NIC * 128], F32))
        xres = [p3.enter_context(nc.sbuf_tensor(f"p3_x{i}", [128, TT3], F32)) for i in range(2)]
        s1fm = p3.enter_context(nc.sbuf_tensor("p3_s1", [128, NOC * TT3], F32))
        s1tm = p3.enter_context(nc.sbuf_tensor("p3_s1t", [128, D], F32))
        ytm = p3.enter_context(nc.sbuf_tensor("p3_ytm", [128, D], F32))
        ystg = [p3.enter_context(nc.sbuf_tensor(f"p3_ys{i}", [128, 512], F32)) for i in range(2)]
        bnst = p3.enter_context(nc.sbuf_tensor("p3_bn", [128, 6 * (D // 512)], F32))
        bnag = p3.enter_context(nc.sbuf_tensor("p3_bna", [128, 2], F32))
        stdb = p3.enter_context(nc.sbuf_tensor("p3_std", [128, 1], F32))
        rstd = p3.enter_context(nc.sbuf_tensor("p3_rstd", [128, 1], F32))
        p_t3 = [p3.enter_context(nc.psum_tensor(f"p3_pt{i}", [128, 512], F32)) for i in range(2)]
        p_w3 = [p3.enter_context(nc.psum_tensor(f"p3_pw{i}", [128, TT3], F32)) for i in range(2)]
        p_y3 = [p3.enter_context(nc.psum_tensor(f"p3_py{i}", [128, 512], F32)) for i in range(2)]

        t_ob = Trk()
        t_ofm = Trk()
        t_wb3 = Trk(2)
        t_w332 = Trk()
        t_wos = Trk()
        t_xr = Trk(2)
        t_s1 = Trk()
        t_s1t = Trk()
        t_ytm = Trk()
        t_ys = Trk(2)
        t_bn = Trk()
        t_bna = Trk()
        t_std = Trk()
        t_rstd = Trk()
        t_pt3 = [Trk() for _ in range(2)]
        t_pw3 = [Trk() for _ in range(2)]
        t_py3 = [Trk() for _ in range(2)]

        for ti in range(T // TT3):
            t0 = ti * TT3
            ldo = pb.add("sync", lambda e, t0=t0: e.dma_start(
                out=ob32.ap().rearrange("p (ic t) -> p ic t", t=TT3),
                in_=o_s.rearrange("(ic p) t -> p ic t", p=128)[:, :, t0:t0 + TT3]),
                deps=t_ob.wr(), dma=True)
            t_ob.did_wr(ldo)
            ofm_wr = []
            for qi in range(4):
                QW3 = NIC * TT3 // 4
                sh = pb.add("scalar", lambda e, qi=qi, QW3=QW3: e.activation(
                    ofh[:, qi * QW3:(qi + 1) * QW3], ob32[:, qi * QW3:(qi + 1) * QW3],
                    AF.Copy, bias=0.0, scale=1.0),
                    deps=[ldo] + (t_ofm.wr() if qi == 0 else []))
                t_ob.did_rd(sh)
                sl = pb.add("vector", lambda e, qi=qi, QW3=QW3: e.tensor_tensor(
                    ofl[:, qi * QW3:(qi + 1) * QW3], ob32[:, qi * QW3:(qi + 1) * QW3],
                    ofh[:, qi * QW3:(qi + 1) * QW3], op=ALU.subtract),
                    deps=[ldo, sh] + (t_ofm.wr() if qi == 0 else []))
                t_ob.did_rd(sl)
                ofm_wr.extend([sh, sl])
            t_ofm.did_wr(ofm_wr[-1])
            t_ofm.r[0].extend(ofm_wr[:-1])
            s1_wr = []
            for oc in range(NOC):
                wp = oc % 2
                if ti == 0:
                    ldw32 = pb.add("sync", lambda e, oc=oc: e.dma_start(
                        out=w332.ap().rearrange("p (ic n) -> p ic n", n=128),
                        in_=wo_d.rearrange("(ic p) n -> p ic n", p=128)[:, :, oc * 128:(oc + 1) * 128]),
                        deps=t_w332.wr(), dma=True)
                    t_w332.did_wr(ldw32)
                    sph = pb.add("scalar", lambda e, wp=wp: e.activation(
                        w3h[wp][:], w332[:], AF.Copy, bias=0.0, scale=1.0),
                        deps=[ldw32] + t_wb3.wr(wp))
                    t_w332.did_rd(sph)
                    spl = pb.add("vector", lambda e, wp=wp: e.tensor_tensor(
                        w3l[wp][:], w332[:], w3h[wp][:], op=ALU.subtract),
                        deps=[ldw32, sph] + t_wb3.wr(wp))
                    t_w332.did_rd(spl)
                    t_wb3.did_wr(spl, wp)
                    t_wb3.r[wp].append(sph)
                    ldw, ldwh = spl, sph
                    sth = pb.add("gpsimd", lambda e, oc=oc, wp=wp: e.dma_start(
                        out=oh_s[oc], in_=w3h[wp][:]), deps=[sph, spl], dma=True)
                    stl = pb.add("gpsimd", lambda e, oc=oc, wp=wp: e.dma_start(
                        out=ol_s[oc], in_=w3l[wp][:]), deps=[spl], dma=True)
                    t_wb3.did_rd(sth, wp)
                    t_wb3.did_rd(stl, wp)
                    t_wos.did_wr(stl)
                    t_wos.r[0].append(sth)
                else:
                    ldwh = pb.add("sync", lambda e, oc=oc, wp=wp: e.dma_start(
                        out=w3h[wp][:], in_=oh_s[oc]),
                        deps=t_wb3.wr(wp) + t_wos.rd(), dma=True)
                    ldw = pb.add("sync", lambda e, oc=oc, wp=wp: e.dma_start(
                        out=w3l[wp][:], in_=ol_s[oc]),
                        deps=t_wb3.wr(wp) + t_wos.rd(), dma=True)
                    t_wos.did_rd(ldwh)
                    t_wos.did_rd(ldw)
                    t_wb3.did_wr(ldw, wp)
                    t_wb3.r[wp].append(ldwh)
                ldx = pb.add("sync", lambda e, oc=oc, t0=t0, wp=wp: e.dma_start(
                    out=xres[wp][:], in_=xf_s[oc * 128:(oc + 1) * 128, t0:t0 + TT3]),
                    deps=t_xr.wr(wp), dma=True)
                t_xr.did_wr(ldx, wp)
                pp = oc % 2
                mms = []
                for ic in range(NIC):
                    for ps_, wb_, ob_ in ((0, w3h, ofh), (1, w3h, ofl), (2, w3l, ofh)):
                        mm = pb.add("tensor", lambda e, ic=ic, pp=pp, wp=wp, wb_=wb_, ob_=ob_, ps_=ps_: e.matmul(
                            p_w3[pp][:], wb_[wp][:, ic * 128:(ic + 1) * 128],
                            ob_[:, ic * TT3:(ic + 1) * TT3],
                            start=(ic == 0 and wb_ is w3h and ob_ is ofh),
                            stop=(ic == NIC - 1 and ps_ == 2)),
                            deps=[ldw, ldwh] + ofm_wr + (t_pw3[pp].wr() if (ic == 0 and ps_ == 0) else []))
                        t_wb3.did_rd(mm, wp)
                        t_ofm.did_rd(mm)
                        mms.append(mm)
                t_pw3[pp].did_wr(mms[-1])
                t_pw3[pp].r[0].extend(mms[:-1])
                ev = pb.add("vector", lambda e, oc=oc, pp=pp, wp=wp: e.tensor_tensor(
                    s1fm[:, oc * TT3:(oc + 1) * TT3], p_w3[pp][:], xres[wp][:], op=ALU.add),
                    deps=mms + [ldx] + (t_s1.wr() if oc == 0 else []))
                t_pw3[pp].did_rd(ev)
                t_xr.did_rd(ev, wp)
                s1_wr.append(ev)
            t_s1.did_wr(s1_wr[-1])
            t_s1.r[0].extend(s1_wr[:-1])
            for sub in range(TSUB3):
                s1t_wr = []
                for g in range(NOC // 4):
                    gp = g % 2
                    trs = []
                    for t4 in range(4):
                        oc = g * 4 + t4
                        tr = pb.add("tensor", lambda e, oc=oc, sub=sub, t4=t4, gp=gp: e.transpose(
                            p_t3[gp][:, t4 * 128:(t4 + 1) * 128],
                            s1fm[:, oc * TT3 + sub * 128: oc * TT3 + (sub + 1) * 128], ident[:, :]),
                            deps=s1_wr + (t_pt3[gp].wr() if t4 == 0 else []))
                        t_s1.did_rd(tr)
                        trs.append(tr)
                    t_pt3[gp].did_wr(trs[-1])
                    t_pt3[gp].r[0].extend(trs[:-1])
                    ev = pb.add("vector", lambda e, g=g, gp=gp: e.tensor_copy(
                        s1tm[:, g * 512:(g + 1) * 512], p_t3[gp][:]),
                        deps=trs + (t_s1t.wr() if g == 0 else []))
                    t_pt3[gp].did_rd(ev)
                    s1t_wr.append(ev)
                t_s1t.did_wr(s1t_wr[-1])
                t_s1t.r[0].extend(s1t_wr[:-1])
                bns = []
                for ch in range(D // 512):
                    bn = pb.add("vector", lambda e, ch=ch: e.bn_stats(
                        bnst[:, ch * 6:(ch + 1) * 6], s1tm[:, ch * 512:(ch + 1) * 512]),
                        deps=s1t_wr + (t_bn.wr() if ch == 0 else []))
                    t_s1t.did_rd(bn)
                    bns.append(bn)
                t_bn.did_wr(bns[-1])
                t_bn.r[0].extend(bns[:-1])
                ag = pb.add("vector", lambda e: e.bn_aggr(bnag[:], bnst[:]),
                            deps=bns + t_bna.wr())
                t_bn.did_rd(ag)
                t_bna.did_wr(ag)
                sq = pb.add("scalar", lambda e: e.activation(
                    stdb[:], bnag[:, 1:2], AF.Sqrt, bias=consts[:, 1:2], scale=1.0),
                    deps=[ag] + t_std.wr() + t_consts.rd())
                t_bna.did_rd(sq)
                t_consts.did_rd(sq)
                t_std.did_wr(sq)
                rc = pb.add("vector", lambda e: e.reciprocal(rstd[:], stdb[:]),
                            deps=[sq] + t_rstd.wr())
                t_std.did_rd(rc)
                t_rstd.did_wr(rc)
                ap = pb.add("vector", lambda e: e.tensor_scalar(
                    ytm[:], s1tm[:], bnag[:, 0:1], rstd[:, 0:1],
                    op0=ALU.subtract, op1=ALU.mult),
                    deps=[ag, rc] + s1t_wr + t_ytm.wr())
                t_s1t.did_rd(ap)
                t_bna.did_rd(ap)
                t_rstd.did_rd(ap)
                t_ytm.did_wr(ap)
                for g in range(NOC // 4):
                    gp = g % 2
                    trs = []
                    for t4 in range(4):
                        oc = g * 4 + t4
                        tr = pb.add("tensor", lambda e, oc=oc, t4=t4, gp=gp: e.transpose(
                            p_y3[gp][:, t4 * 128:(t4 + 1) * 128],
                            ytm[:, oc * 128:(oc + 1) * 128], ident[:, :]),
                            deps=[ap] + (t_py3[gp].wr() if t4 == 0 else []))
                        t_ytm.did_rd(tr)
                        trs.append(tr)
                    t_py3[gp].did_wr(trs[-1])
                    t_py3[gp].r[0].extend(trs[:-1])
                    ev = pb.add("vector", lambda e, gp=gp: e.tensor_copy(ystg[gp][:], p_y3[gp][:]),
                                deps=trs + t_ys.wr(gp))
                    t_py3[gp].did_rd(ev)
                    t_ys.did_wr(ev, gp)
                    st = pb.add("gpsimd", lambda e, g=g, t0=t0, sub=sub, gp=gp: e.dma_start(
                        out=y_s.rearrange("(oc p) t -> p oc t", p=128)
                        [:, g * 4:(g + 1) * 4, t0 + sub * 128: t0 + (sub + 1) * 128],
                        in_=ystg[gp].ap().rearrange("p (oc t) -> p oc t", t=128)),
                        deps=[ev], dma=True)
                    t_ys.did_rd(st, gp)
        pb.barrier()

    TT4 = min(TT, 256)
    TSUB4 = TT4 // 128
    with ExitStack() as p4:
        yfm = p4.enter_context(nc.sbuf_tensor("p4_y", [128, NIC * TT4], F32))
        yfh = p4.enter_context(nc.sbuf_tensor("p4_yh", [128, NIC * TT4], BF16))
        yfl = p4.enter_context(nc.sbuf_tensor("p4_yl", [128, NIC * TT4], BF16))
        w4h = [p4.enter_context(nc.sbuf_tensor(f"p4_wh{i}", [128, NIC * 128], BF16)) for i in range(2)]
        w4l = [p4.enter_context(nc.sbuf_tensor(f"p4_wl{i}", [128, NIC * 128], BF16)) for i in range(2)]
        asb = p4.enter_context(nc.sbuf_tensor("p4_a", [R, TT4], F32))
        zt0 = [p4.enter_context(nc.sbuf_tensor(f"p4_z{i}", [128, TT4], F32)) for i in range(2)]
        s2fm = p4.enter_context(nc.sbuf_tensor("p4_s2", [128, NOC * TT4], F32))
        s2tm = p4.enter_context(nc.sbuf_tensor("p4_s2t", [128, D], F32))
        outtm = p4.enter_context(nc.sbuf_tensor("p4_ot", [128, D], F32))
        bnst4 = p4.enter_context(nc.sbuf_tensor("p4_bn", [128, 6 * (D // 512)], F32))
        bnag4 = p4.enter_context(nc.sbuf_tensor("p4_bna", [128, 2], F32))
        stdb4 = p4.enter_context(nc.sbuf_tensor("p4_std", [128, 1], F32))
        rstd4 = p4.enter_context(nc.sbuf_tensor("p4_rstd", [128, 1], F32))
        p_z4 = [p4.enter_context(nc.psum_tensor(f"p4_pz{i}", [128, TT4], F32)) for i in range(2)]
        p_l4 = p4.enter_context(nc.psum_tensor("p4_pl", [128, TT4], F32))
        p_a4 = p4.enter_context(nc.psum_tensor("p4_pa", [128, TT4], F32))
        p_t4 = [p4.enter_context(nc.psum_tensor(f"p4_pt{i}", [128, 512], F32)) for i in range(2)]

        t_y4 = Trk()
        t_yhl = Trk()
        t_wb4 = Trk(2)
        t_a4 = Trk()
        t_z0 = Trk(2)
        t_s2 = Trk()
        t_s2t = Trk()
        t_ot = Trk()
        t_bn4 = Trk()
        t_bna4 = Trk()
        t_std4 = Trk()
        t_rstd4 = Trk()
        t_pz4 = [Trk() for _ in range(2)]
        t_pl4 = Trk()
        t_pa4 = Trk()
        t_pt4 = [Trk() for _ in range(2)]

        out_sts = []
        for ti in range(T // TT4):
            t0 = ti * TT4
            ldy = pb.add("sync", lambda e, t0=t0: e.dma_start(
                out=yfm.ap().rearrange("p (ic t) -> p ic t", t=TT4),
                in_=y_s.rearrange("(ic p) t -> p ic t", p=128)[:, :, t0:t0 + TT4]),
                deps=t_y4.wr(), dma=True)
            t_y4.did_wr(ldy)
            yh = pb.add("scalar", lambda e: e.activation(
                yfh[:], yfm[:], AF.Copy, bias=0.0, scale=1.0),
                deps=[ldy] + t_yhl.wr())
            t_y4.did_rd(yh)
            yl = pb.add("vector", lambda e: e.tensor_tensor(
                yfl[:], yfm[:], yfh[:], op=ALU.subtract), deps=[ldy, yh])
            t_y4.did_rd(yl)
            t_yhl.did_wr(yl)
            t_yhl.r[0].append(yh)
            ams = []
            for ic in range(NIC):
                mm = pb.add("tensor", lambda e, ic=ic: e.matmul(
                    p_a4[:R, :], awlt[:, ic * R:(ic + 1) * R], yfm[:, ic * TT4:(ic + 1) * TT4],
                    start=(ic == 0), stop=(ic == NIC - 1)),
                    deps=[ldy] + t_awlt.rd() + (t_pa4.wr() if ic == 0 else []))
                t_y4.did_rd(mm)
                t_awlt.did_rd(mm)
                ams.append(mm)
            t_pa4.did_wr(ams[-1])
            t_pa4.r[0].extend(ams[:-1])
            eva = pb.add("vector", lambda e: e.tensor_copy(asb[:R, :], p_a4[:R, :]),
                         deps=ams + t_a4.wr())
            t_pa4.did_rd(eva)
            t_a4.did_wr(eva)
            s2_wr = []
            for oc in range(NOC):
                wp = oc % 2
                ldwh = pb.add("sync", lambda e, oc=oc, wp=wp: e.dma_start(
                    out=w4h[wp][:], in_=wbth_s[oc]),
                    deps=t_wb4.wr(wp), dma=True)
                ldw = pb.add("sync", lambda e, oc=oc, wp=wp: e.dma_start(
                    out=w4l[wp][:], in_=wbtl_s[oc]),
                    deps=t_wb4.wr(wp), dma=True)
                t_wb4.did_wr(ldw, wp)
                t_wb4.r[wp].append(ldwh)
                pp = oc % 2
                mms = []
                for ic in range(NIC):
                    for ps_, wb_, yb_ in ((0, w4h, yfh), (1, w4h, yfl), (2, w4l, yfh)):
                        mm = pb.add("tensor", lambda e, ic=ic, pp=pp, wp=wp, wb_=wb_, yb_=yb_, ps_=ps_: e.matmul(
                            p_z4[pp][:], wb_[wp][:, ic * 128:(ic + 1) * 128],
                            yb_[:, ic * TT4:(ic + 1) * TT4],
                            start=(ic == 0 and wb_ is w4h and yb_ is yfh),
                            stop=(ic == NIC - 1 and ps_ == 2)),
                            deps=[ldw, ldwh, yh, yl] + (t_pz4[pp].wr() if (ic == 0 and ps_ == 0) else []))
                        t_wb4.did_rd(mm, wp)
                        t_yhl.did_rd(mm)
                        mms.append(mm)
                t_pz4[pp].did_wr(mms[-1])
                t_pz4[pp].r[0].extend(mms[:-1])
                lm = pb.add("tensor", lambda e, oc=oc: e.matmul(
                    p_l4[:], bwrt[:R, oc * 128:(oc + 1) * 128], asb[:R, :],
                    start=True, stop=True),
                    deps=[eva] + t_bwrt.rd() + t_pl4.wr())
                t_a4.did_rd(lm)
                t_bwrt.did_rd(lm)
                t_pl4.did_wr(lm)
                e0 = pb.add("scalar", lambda e, oc=oc, pp=pp, wp=wp: e.activation(
                    zt0[wp][:], p_z4[pp][:], AF.Identity, bias=b_sb[:, oc:oc + 1], scale=1.0),
                    deps=mms + t_bsb.rd() + t_z0.wr(wp))
                t_pz4[pp].did_rd(e0)
                t_bsb.did_rd(e0)
                t_z0.did_wr(e0, wp)
                e1 = pb.add("vector", lambda e, oc=oc, wp=wp: e.scalar_tensor_tensor(
                    s2fm[:, oc * TT4:(oc + 1) * TT4], p_l4[:], 0.25, zt0[wp][:],
                    op0=ALU.mult, op1=ALU.add),
                    deps=[lm, e0] + (t_s2.wr() if oc == 0 else []))
                t_pl4.did_rd(e1)
                t_z0.did_rd(e1, wp)
                e2 = pb.add("vector", lambda e, oc=oc: e.tensor_tensor(
                    s2fm[:, oc * TT4:(oc + 1) * TT4], s2fm[:, oc * TT4:(oc + 1) * TT4],
                    yfm[:, oc * TT4:(oc + 1) * TT4], op=ALU.add),
                    deps=[e1])
                t_y4.did_rd(e2)
                s2_wr.append(e2)
            t_s2.did_wr(s2_wr[-1])
            t_s2.r[0].extend(s2_wr[:-1])
            for sub in range(TSUB4):
                s2t_wr = []
                for g in range(NOC // 4):
                    gp = g % 2
                    trs = []
                    for t4 in range(4):
                        oc = g * 4 + t4
                        tr = pb.add("tensor", lambda e, oc=oc, sub=sub, t4=t4, gp=gp: e.transpose(
                            p_t4[gp][:, t4 * 128:(t4 + 1) * 128],
                            s2fm[:, oc * TT4 + sub * 128: oc * TT4 + (sub + 1) * 128], ident[:, :]),
                            deps=s2_wr + (t_pt4[gp].wr() if t4 == 0 else []))
                        t_s2.did_rd(tr)
                        trs.append(tr)
                    t_pt4[gp].did_wr(trs[-1])
                    t_pt4[gp].r[0].extend(trs[:-1])
                    ev = pb.add("vector", lambda e, g=g, gp=gp: e.tensor_copy(
                        s2tm[:, g * 512:(g + 1) * 512], p_t4[gp][:]),
                        deps=trs + (t_s2t.wr() if g == 0 else []))
                    t_pt4[gp].did_rd(ev)
                    s2t_wr.append(ev)
                t_s2t.did_wr(s2t_wr[-1])
                t_s2t.r[0].extend(s2t_wr[:-1])
                bns = []
                for ch in range(D // 512):
                    bn = pb.add("vector", lambda e, ch=ch: e.bn_stats(
                        bnst4[:, ch * 6:(ch + 1) * 6], s2tm[:, ch * 512:(ch + 1) * 512]),
                        deps=s2t_wr + (t_bn4.wr() if ch == 0 else []))
                    t_s2t.did_rd(bn)
                    bns.append(bn)
                t_bn4.did_wr(bns[-1])
                t_bn4.r[0].extend(bns[:-1])
                ag = pb.add("vector", lambda e: e.bn_aggr(bnag4[:], bnst4[:]),
                            deps=bns + t_bna4.wr())
                t_bn4.did_rd(ag)
                t_bna4.did_wr(ag)
                sq = pb.add("scalar", lambda e: e.activation(
                    stdb4[:], bnag4[:, 1:2], AF.Sqrt, bias=consts[:, 1:2], scale=1.0),
                    deps=[ag] + t_std4.wr() + t_consts.rd())
                t_bna4.did_rd(sq)
                t_consts.did_rd(sq)
                t_std4.did_wr(sq)
                rc = pb.add("vector", lambda e: e.reciprocal(rstd4[:], stdb4[:]),
                            deps=[sq] + t_rstd4.wr())
                t_std4.did_rd(rc)
                t_rstd4.did_wr(rc)
                ap = pb.add("vector", lambda e: e.tensor_scalar(
                    outtm[:], s2tm[:], bnag4[:, 0:1], rstd4[:, 0:1],
                    op0=ALU.subtract, op1=ALU.mult),
                    deps=[ag, rc] + s2t_wr + t_ot.wr())
                t_s2t.did_rd(ap)
                t_bna4.did_rd(ap)
                t_rstd4.did_rd(ap)
                t_ot.did_wr(ap)
                st = pb.add("gpsimd", lambda e, t0=t0, sub=sub: e.dma_start(
                    out=out_d[t0 + sub * 128: t0 + (sub + 1) * 128, :], in_=outtm[:]),
                    deps=[ap], dma=True)
                t_ot.did_rd(st)
                out_sts.append(st)

        fin = pb.add("sync", lambda e: e.nop(), deps=out_sts)

    pb.emit()
    glob.close()
    nc.finalize()
    return nc, c, mask_cols, mask_plan


_NC_CACHE = {}


def _get_nc(cfg_key_items):
    key = tuple(sorted(cfg_key_items))
    if key not in _NC_CACHE:
        _NC_CACHE[key] = build_nc(dict(cfg_key_items))
    return _NC_CACHE[key]


def make_core_inputs(cfg, full, b, h):
    c = _derived(cfg)
    T, W, KV, D = c["T"], c["W"], c["KV"], c["D"]
    mask_cols, mask_plan = _mask_layout(c)
    T0 = h * T
    x = full["x"][b]
    if h == 0:
        xin = np.concatenate([np.zeros((W, D), np.float32), np.ascontiguousarray(x[0:T])], axis=0)
    else:
        xin = np.ascontiguousarray(x[T0 - W: T0 + T])
    cq, sq, ck, sk = build_rope_tables(c, T0)
    msk = build_mask_values(c, T0, mask_cols, mask_plan)
    rot = np.zeros((128, 128), np.float32)
    rot[np.arange(64) + 64, np.arange(64)] = 1.0
    rot[np.arange(64), np.arange(64) + 64] = 1.0
    return {
        "xin": xin,
        "wq": full["wq"], "wk": full["wk"], "wv": full["wv"], "wo": full["wo"],
        "wb": full["wb"], "bb": full["bb"].reshape(c["NOC"], 128),
        "Ain": full["A"], "WLin": full["WL"], "Bin": full["B"], "WRin": full["WR"],
        "msk": msk, "cq": cq, "sq": sq, "ck": ck, "sk": sk,
        "ident": np.eye(128, dtype=np.float32), "rotm": rot,
    }


def kernel(L0_attn_x, L0_attn_wq, L0_attn_wk, L0_attn_wv, L0_attn_wo,
           L0_ffn_W_base, L0_ffn_b_base, L0_ffn_A, L0_ffn_B, L0_ffn_WL, L0_ffn_WR):
    cfg = CFG_REAL
    c = _derived(cfg)
    nc, _, _, _ = _get_nc(tuple(cfg.items()))
    full = dict(
        x=np.ascontiguousarray(L0_attn_x, np.float32),
        wq=np.ascontiguousarray(L0_attn_wq, np.float32),
        wk=np.ascontiguousarray(L0_attn_wk, np.float32),
        wv=np.ascontiguousarray(L0_attn_wv, np.float32),
        wo=np.ascontiguousarray(L0_attn_wo, np.float32),
        wb=np.ascontiguousarray(L0_ffn_W_base, np.float32),
        bb=np.ascontiguousarray(L0_ffn_b_base, np.float32),
        A=np.ascontiguousarray(L0_ffn_A, np.float32),
        WL=np.ascontiguousarray(L0_ffn_WL, np.float32),
        B=np.ascontiguousarray(L0_ffn_B, np.float32),
        WR=np.ascontiguousarray(L0_ffn_WR, np.float32),
    )
    in_maps = []
    for core in range(8):
        b, h = core // 2, core % 2
        in_maps.append(make_core_inputs(cfg, full, b, h))
    res = run_bass_kernel_spmd(nc, in_maps, core_ids=list(range(8)))
    T, D, B = c["T"], c["D"], cfg["B"]
    outp = np.empty((B, cfg["S"], D), np.float32)
    for core in range(8):
        b, h = core // 2, core % 2
        outp[b, h * T:(h + 1) * T] = res.results[core]["out"]
    return outp
